# version 02
# baseline (speedup 1.0000x reference)
from contextlib import ExitStack

import numpy as np
import ml_dtypes

import concourse.bass as bass
import concourse.mybir as mybir
from concourse.alu_op_type import AluOpType as ALU
from concourse.bass_utils import run_bass_kernel_spmd

F32 = mybir.dt.float32
BF16 = mybir.dt.bfloat16
AF = mybir.ActivationFunctionType
AX = mybir.AxisListType
BF = ml_dtypes.bfloat16

D = 1024
TSEQ = 2048
BLOC = 2
HT = 1024
HS = 2048
NTILE = 16
NNT = 4
NDC = 16
NLC = 8
NEI = 9
NFB = 4
KTOP = 65536.0
EPS = 1e-5
GC_A = 0.044715
GC_S = 1.5957691216057308
NBIS = 26

SEMS = [
    "g", "q", "cc",
    "pB", "pC", "pD", "pU", "pY", "pUD", "pT", "pG", "pW1", "pW2", "pX",
    "aA", "aU", "aM", "aG", "aS", "aX",
    "vB", "vS", "vC", "vY", "vT", "vM", "vZ", "vH", "vA", "vX",
]


class Ctx:
    def __init__(self, nc, stk):
        self.nc = nc
        self.h = {n: stk.enter_context(nc.semaphore(n)) for n in SEMS}
        self.c = {n: 0 for n in SEMS}
        self.mk = {}

    def snap(self):
        return dict(self.c)

    def load(self, s):
        self.c = dict(s)
        self.mk = {}

    def inc(self, inst, sem, k=1):
        self.c[sem] += k
        if inst is not None:
            inst.then_inc(self.h[sem], k)
        return inst

    def mark(self, sem, key):
        self.mk[(sem,) + key] = self.c[sem]

    def wait(self, E, sem, key=None, val=None):
        if val is None:
            k = (sem,) + key
            if k not in self.mk:
                return
            val = self.mk[k]
        if E is not None and val > 0:
            E.wait_ge(self.h[sem], val)



def _host_prep(W, b, A, Dm, gw, W1, b1, W2, b2, Ws1, bs1, Ws2, bs2):
    WT = np.ascontiguousarray(W.T)
    p = np.arange(128)

    winp = np.empty((NDC, 128, 17408), np.float32)
    for dc in range(NDC):
        cols = []
        rows_ud = np.concatenate([np.arange(dc * 64, dc * 64 + 64),
                                  1024 + np.arange(dc * 64, dc * 64 + 64)])
        cols.append(np.concatenate(list(WT[:, rows_ud].reshape(8, 128, 128)), axis=1))
        for L in range(NLC):
            dvec = dc * 64 + L * 8 + p // 16
            for base in (2048, 18432):
                rows = base + dvec * 16 + p % 16
                cols.append(np.concatenate(list(WT[:, rows].reshape(8, 128, 128)),
                                           axis=1))
        winp[dc] = np.concatenate(cols, axis=1)

    wffn = np.empty((NEI * NFB, 128, 16384), np.float32)
    for ei in range(NEI):
        W1e = Ws1 if ei == 0 else W1[ei - 1]
        W2e = Ws2 if ei == 0 else W2[ei - 1]
        for fb in range(NFB):
            cols = []
            for j in range(8):
                blk = W1e[:, fb * 1024 + j * 128: fb * 1024 + (j + 1) * 128]
                cols.append(np.concatenate(list(blk.reshape(8, 128, 128)), axis=1))
            for m in range(8):
                cols.append(np.concatenate(
                    [W2e[fb * 1024 + k * 128: fb * 1024 + (k + 1) * 128,
                         m * 128:(m + 1) * 128] for k in range(8)], axis=1))
            wffn[ei * NFB + fb] = np.concatenate(cols, axis=1)

    bud = np.empty((128, NDC), np.float32)
    bBC = np.empty((128, 256), np.float32)
    acol = np.empty((128, 128), np.float32)
    dcol = np.empty((128, NDC), np.float32)
    for dc in range(NDC):
        bud[0:64, dc] = b[dc * 64: dc * 64 + 64]
        bud[64:128, dc] = b[1024 + dc * 64: 1024 + dc * 64 + 64]
        dcol[:, dc] = Dm[dc * 64 + (p % 64)]
        for L in range(NLC):
            dvec = dc * 64 + L * 8 + p // 16
            bBC[:, dc * 8 + L] = b[2048 + dvec * 16 + p % 16]
            bBC[:, 128 + dc * 8 + L] = b[18432 + dvec * 16 + p % 16]
            acol[:, dc * 8 + L] = A[dvec, p % 16]

    b1c = np.empty((128, NEI * 32), np.float32)
    for ei in range(NEI):
        b1e = bs1 if ei == 0 else b1[ei - 1]
        for fb in range(NFB):
            for j in range(8):
                b1c[:, ei * 32 + fb * 8 + j] = b1e[fb * 1024 + j * 128:
                                                   fb * 1024 + (j + 1) * 128]

    b2bs2 = np.empty((9, 1024), np.float32)
    b2bs2[0:8] = b2
    b2bs2[8] = bs2

    gwh = np.empty((128, 64), np.float32)
    for k in range(8):
        gwh[:, k * 8:(k + 1) * 8] = gw[:, k * 128:(k + 1) * 128].T

    esel = np.zeros((128, 8 * 128), np.float32)
    ssel = np.zeros((128, 8 * 128), np.float32)
    for L in range(NLC):
        for q in range(128):
            esel[(L * 8 + q // 16) % 64, L * 128 + q] = 1
            esel[64 + (L * 8 + q // 16) % 64, L * 128 + q] = 1
        for pp in range(128):
            m = L * 8 + pp // 16
            ssel[pp, L * 128 + m] = 1
            ssel[pp, L * 128 + 64 + m] = 1
    e9sel = np.zeros((9, 8 * 128), np.float32)
    for e in range(8):
        e9sel[e, e * 128:(e + 1) * 128] = 1
    identf = np.eye(128, dtype=np.float32)
    cst = np.zeros((128, 8), np.float32)
    for i, vv in enumerate([EPS, -0.5, 0.5, GC_A, GC_S, -1.0, 1.0, 0.0]):
        cst[:, i] = vv
    ones4096 = np.ones((1, 4096), np.float32)

    return dict(winp=winp, wffn=wffn, bud=bud, bBC=bBC, acol=acol, dcol=dcol,
                b1c=b1c, b2bs2=b2bs2, gwh=gwh, esel=esel, ssel=ssel,
                e9sel=e9sel, identf=identf, cst=cst, ones4096=ones4096)



def build_nc():
    nc = bass.Bass("TRN2", target_bir_lowering=False)

    X = nc.dram_tensor("x", [BLOC, TSEQ, D], F32, kind="ExternalInput").ap()
    WINP = nc.dram_tensor("winp", [NDC, 128, 17408], F32, kind="ExternalInput").ap()
    WFFN = nc.dram_tensor("wffn", [NEI * NFB, 128, 16384], F32, kind="ExternalInput").ap()
    BUD = nc.dram_tensor("bud", [128, NDC], F32, kind="ExternalInput").ap()
    BBC = nc.dram_tensor("bBC", [128, 256], F32, kind="ExternalInput").ap()
    ACOL = nc.dram_tensor("acol", [128, 128], F32, kind="ExternalInput").ap()
    DCOL = nc.dram_tensor("dcol", [128, NDC], F32, kind="ExternalInput").ap()
    B1C = nc.dram_tensor("b1c", [128, NEI * 32], F32, kind="ExternalInput").ap()
    B2BS2 = nc.dram_tensor("b2bs2", [9, 1024], F32, kind="ExternalInput").ap()
    GWH = nc.dram_tensor("gwh", [128, 64], F32, kind="ExternalInput").ap()
    ESEL = nc.dram_tensor("esel", [128, 1024], F32, kind="ExternalInput").ap()
    SSEL = nc.dram_tensor("ssel", [128, 1024], F32, kind="ExternalInput").ap()
    E9SEL = nc.dram_tensor("e9sel", [9, 1024], F32, kind="ExternalInput").ap()
    IDENTF = nc.dram_tensor("identf", [128, 128], F32, kind="ExternalInput").ap()
    CST = nc.dram_tensor("cst", [128, 8], F32, kind="ExternalInput").ap()
    ONES4096 = nc.dram_tensor("ones4096", [1, 4096], F32, kind="ExternalInput").ap()
    OUT = nc.dram_tensor("out", [BLOC, TSEQ, D], F32, kind="ExternalOutput").ap()

    ydram = nc.dram_tensor("ydram", [NTILE, 128, 1024], BF16).ap()
    xn2sp = nc.dram_tensor("xn2sp", [8, 128, HS], BF16).ap()
    cc_in_t = nc.dram_tensor("cc_in", [128, 256], F32)
    cc_out_t = nc.dram_tensor("cc_out", [1024, 256], F32)

    with ExitStack() as stk:
        E_ = stk.enter_context
        A2 = E_(nc.sbuf_tensor("A2", [128, 16384], BF16))
        A3 = E_(nc.sbuf_tensor("A3", [128, 16384], BF16))
        A4 = E_(nc.sbuf_tensor("A4", [128, 16384], BF16))
        AW = E_(nc.sbuf_tensor("AW", [128, 10240], BF16))
        AF32 = E_(nc.sbuf_tensor("AF32", [128, 5120], F32))
        AG = E_(nc.sbuf_tensor("AG", [128, 8192], BF16))
        AM = E_(nc.sbuf_tensor("AM", [128, 6144], BF16))
        ysb = E_(nc.sbuf_tensor("ysb", [128, 2048], F32))
        ytstg = E_(nc.sbuf_tensor("ytstg", [128, 1024], BF16))
        lgs = E_(nc.sbuf_tensor("lgs", [128, 256], F32))
        rows = E_(nc.sbuf_tensor("rows", [128, 16], F32))
        m2s = E_(nc.sbuf_tensor("m2s", [128, 32], F32))
        s2s = E_(nc.sbuf_tensor("s2s", [128, 32], F32))
        hstA = E_(nc.sbuf_tensor("hstA", [128, 128], F32))
        hstB = E_(nc.sbuf_tensor("hstB", [128, 128], F32))
        ones128 = E_(nc.sbuf_tensor("ones128", [128, 128], F32))
        bis = E_(nc.sbuf_tensor("bis", [128, 32], F32))
        buds = E_(nc.sbuf_tensor("buds", [128, NDC], F32))
        bBCs = E_(nc.sbuf_tensor("bBCs", [128, 256], F32))
        acols = E_(nc.sbuf_tensor("acols", [128, 128], F32))
        dcols = E_(nc.sbuf_tensor("dcols", [128, NDC], F32))
        b1cs = E_(nc.sbuf_tensor("b1cs", [128, NEI * 32], F32))
        b2s = E_(nc.sbuf_tensor("b2s", [9, 1024], BF16))
        gws = E_(nc.sbuf_tensor("gws", [128, 64], BF16))
        esels = E_(nc.sbuf_tensor("esels", [128, 1024], BF16))
        ssels = E_(nc.sbuf_tensor("ssels", [128, 1024], BF16))
        e9s = E_(nc.sbuf_tensor("e9s", [9, 1024], BF16))
        idf = E_(nc.sbuf_tensor("idf", [128, 128], F32))
        csts = E_(nc.sbuf_tensor("csts", [128, 8], F32))
        abar = E_(nc.sbuf_tensor("abar", [128, 1024], F32))
        ubcs = E_(nc.sbuf_tensor("ubcs", [128, 1024], F32))
        busb = E_(nc.sbuf_tensor("busb", [128, 512], F32))
        hsb = E_(nc.sbuf_tensor("hsb", [128, 1024], F32))
        spsb = E_(nc.sbuf_tensor("spsb", [128, 1024], F32))
        pB = [E_(nc.psum_tensor(f"pB{i}", [128, 512], F32)) for i in range(2)]
        pC = [E_(nc.psum_tensor(f"pC{i}", [128, 512], F32)) for i in range(2)]
        pD = E_(nc.psum_tensor("pD", [128, 512], F32))
        pS = E_(nc.psum_tensor("pS", [128, 512], F32))
        pY = [E_(nc.psum_tensor(f"pY{i}", [128, 512], F32)) for i in range(2)]

        ctx = Ctx(nc, stk)
        block = E_(nc.Block())

        def XN1(k):
            return A2[:, k * 2048:(k + 1) * 2048]

        def XN2(k):
            return A3[:, k * 2048:(k + 1) * 2048]

        def HBUF(j):
            return A2[:, j * 2048:(j + 1) * 2048]

        def OUTH(m):
            return A4[:, m * 2048:(m + 1) * 2048]

        u2 = AM[:, 0:2048]
        d2 = AM[:, 2048:4096]
        gate9 = AM[0:9, 0:4096]
        gbc = AM[:, 4096:6144]
        xstg = [AG[:, 0:1024], AG[:, 1024:2048]]
        ystg = [AG[:, 2048:3072], AG[:, 3072:4096]]
        cntjunk = AG[:, 2048:4096]
        chsb = [AG[:, 4096 + i * 512: 4096 + (i + 1) * 512] for i in range(2)]
        xbV = [AG[:, 0:2048], AG[:, 2048:4096]]
        sqV = [AG[:, 4096:6144], AG[:, 6144:8192]]
        wrL = [AW[:, s * 2048:(s + 1) * 2048] for s in range(4)]
        wrUD = [AW[:, 8192 + s * 1024: 8192 + (s + 1) * 1024] for s in range(2)]
        w1r = [AW[:, s * 1024:(s + 1) * 1024] for s in range(4)]
        w2r = [AW[:, 4096 + s * 1024: 4096 + (s + 1) * 1024] for s in range(4)]
        xn1f = [AF32[:, 0:1024], AF32[:, 1024:2048]]
        x1f = xn1f
        gath = AF32[:, 0:2048]
        trsrc = AF32[:, 0:1024]
        x1fin = AF32[:, 1024:2048]
        outtmV = [AF32[:, 2048:3072], AF32[:, 4096:5120]]
        STATJ = AF32[:, 3072:4096]
        mabar = [abar[:, 0:512], abar[:, 512:1024]]
        mubc = [ubcs[:, 0:512], ubcs[:, 512:1024]]
        mh = [hsb[:, 0:512], hsb[:, 512:1024]]
        spr = spsb[:, 0:512]
        spa = spsb[:, 512:1024]

        R_M1, R_R1, R_VAR, R_TMP, R_MY, R_RY, R_M3, R_R3 = range(8)

        def rc(i):
            return rows[:, i:i + 1]

        def csc(i):
            return csts[:, i:i + 1]

        C_EPS, C_NHALF, C_PHALF, C_GA, C_GS, C_NEG1, C_ONE = range(7)

        b_lo = bis[0:1, 0:1]
        b_hi = bis[0:1, 1:2]
        b_tau = bis[0:1, 2:3]
        b_m = bis[0:1, 3:4]
        b_t1 = bis[0:1, 4:5]
        b_t2 = bis[0:1, 5:6]
        b_tot = bis[0:1, 6:7]
        b_tcol = bis[:, 8:9]
        b_cnt = bis[:, 10:11]
        sm_e = bis[:, 16:24]
        sm_g = bis[:, 24:32]

        def x_tile_ap(h, tt):
            b = tt // 8
            t0 = h * HT + (tt % 8) * 128
            return X[b, t0:t0 + 128, :]

        def out_tile_ap(h, tt):
            b = tt // 8
            t0 = h * HT + (tt % 8) * 128
            return OUT[b, t0:t0 + 128, :]

        def gidx(dc, L, nt):
            return (dc * NLC + L) * NNT + nt

        def vg(v, inst, sem="vM"):
            ctx.inc(inst, sem, 1)
            if v is not None:
                v.wait_ge(ctx.h[sem], ctx.c[sem])

        def ln_stats(v, src_ap, mcol):
            n = src_ap.shape[-1] if v is not None else 0
            if v is None:
                ctx.c["vM"] += 7
                return
            vg(v, v.reduce_sum(rc(R_TMP), src_ap, AX.X))
            vg(v, v.tensor_tensor(STATJ[:, 0:n], src_ap, src_ap, ALU.mult))
            vg(v, v.reduce_sum(rc(R_VAR), STATJ[:, 0:n], AX.X))
            vg(v, v.tensor_single_scalar(mcol, rc(R_TMP), 1.0 / D, ALU.mult))
            vg(v, v.tensor_single_scalar(rc(R_VAR), rc(R_VAR), 1.0 / D, ALU.mult))
            vg(v, v.tensor_tensor(rc(R_TMP), mcol, mcol, ALU.mult))
            vg(v, v.tensor_tensor(rc(R_VAR), rc(R_VAR), rc(R_TMP), ALU.subtract))

        def ph_init(ctx, en, E):
            g = E if en == "gpsimd" else None
            q = E if en == "sync" else None
            v = E if en == "vector" else None
            pairs_q = [(buds[:], BUD), (bBCs[:], BBC), (acols[:], ACOL),
                       (dcols[:], DCOL), (b1cs[:], B1C), (idf[:], IDENTF),
                       (csts[:], CST)]
            for dst, src in pairs_q:
                ctx.inc(q and q.dma_start(out=dst, in_=src), "q", 16)
            pairs_g = [(b2s[:], B2BS2), (gws[:], GWH), (esels[:], ESEL),
                       (ssels[:], SSEL), (e9s[:], E9SEL)]
            for dst, src in pairs_g:
                ctx.inc(g and g.dma_start(out=dst, in_=src), "g", 16)
            ctx.mark("q", ("cst",))
            ctx.mark("g", ("cst",))
            if v:
                v.memset(ones128[:], 1.0)
                v.memset(hstA[:], 0.0)
                ctx.inc(v.memset(hstB[:], 0.0), "vM", 1)
            else:
                ctx.c["vM"] += 1

        def ph_ln1(ctx, en, E, h):
            import os as _os
            DBG = _os.environ.get("MK_DBG", "")
            g = E if en == "gpsimd" else None
            v = E if en == "vector" else None
            a = E if en == "scalar" else None
            t = E if en == "tensor" else None
            for tt in range(NTILE):
                if g:
                    ctx.wait(g, "vM", ("xdone", h, tt - 2))
                ctx.inc(g and g.dma_start(out=xstg[tt % 2][:],
                                          in_=x_tile_ap(h, tt)), "g", 16)
                ctx.mark("g", ("x", h, tt))
                if v:
                    ctx.wait(v, "g", ("x", h, tt))
                    ctx.wait(v, "aM", ("r1", h, tt - 1))
                ln_stats(v, xstg[tt % 2][:] if v else None, rc(R_M1))
                ctx.mark("vM", ("st1", h, tt))
                if a:
                    if tt == 0:
                        ctx.wait(a, "q", ("cst",))
                    ctx.wait(a, "vM", ("st1", h, tt))
                    a.activation(rc(R_TMP), rc(R_VAR), AF.Ln, bias=csc(C_EPS))
                    ctx.inc(a.activation(rc(R_R1), rc(R_TMP), AF.Exp,
                                         scale=csc(C_NHALF)), "aM", 1)
                else:
                    ctx.c["aM"] += 1
                ctx.mark("aM", ("r1", h, tt))
                if v:
                    ctx.wait(v, "aM", ("r1", h, tt))
                    ctx.wait(v, "pT", ("tr", h, tt - 2))
                    ctx.inc(v.tensor_scalar(xn1f[tt % 2][:], xstg[tt % 2][:],
                                            rc(R_M1), rc(R_R1),
                                            ALU.subtract, ALU.mult), "vM", 1)
                else:
                    ctx.c["vM"] += 1
                ctx.mark("vM", ("xn1f", h, tt))
                ctx.mark("vM", ("xdone", h, tt))
                if DBG == "xstg":
                    if en == "gpsimd":
                        ctx.wait(E, "g", ("x", h, tt))
                        ctx.inc(E.dma_start(out=out_tile_ap(h, tt),
                                            in_=xstg[tt % 2][:]), "g", 16)
                    else:
                        ctx.c["g"] += 16
                    ctx.mark("g", ("dbgxs", h, tt))
                    if v:
                        ctx.wait(v, "g", ("dbgxs", h, tt))
                if DBG == "xn1":
                    if q_ := None:
                        pass
                    if en == "sync":
                        ctx.wait(E, "vM", ("xn1f", h, tt))
                    ctx.inc(E.dma_start(out=out_tile_ap(h, tt),
                                        in_=xn1f[tt % 2][:]) if en == "sync" else None,
                            "q", 16)
                    ctx.mark("q", ("dbgxn1", h, tt))
                    if v:
                        ctx.wait(v, "q", ("dbgxn1", h, tt))
                if t:
                    ctx.wait(t, "vM", ("xn1f", h, tt))
                    ctx.wait(t, "vT", ("trev", h, tt - 1))
                    last = None
                    for jj in range(8):
                        bank = pY[jj // 4]
                        last = t.transpose(bank[:, (jj % 4) * 128:(jj % 4 + 1) * 128],
                                           xn1f[tt % 2][:, jj * 128:(jj + 1) * 128],
                                           idf[:])
                    ctx.inc(last, "pT", 1)
                else:
                    ctx.c["pT"] += 1
                ctx.mark("pT", ("tr", h, tt))
                if v:
                    ctx.wait(v, "pT", ("tr", h, tt))
                    last = None
                    for jj in range(8):
                        bank = pY[jj // 4]
                        last = v.tensor_copy(XN1(jj)[:, tt * 128:(tt + 1) * 128],
                                             bank[:, (jj % 4) * 128:(jj % 4 + 1) * 128])
                    ctx.inc(last, "vT", 1)
                else:
                    ctx.c["vT"] += 1
                ctx.mark("vT", ("trev", h, tt))

        def ph_mamba(ctx, en, E, h):
            import os as _os
            DBG = _os.environ.get("MK_DBG", "")
            NDCR = min(NDC, int(_os.environ.get("MK_DCS", str(NDC))))
            SKIP = _os.environ.get("MK_SKIP", "")
            GLVL = int(_os.environ.get("MK_GRP", "6"))
            ELVL = int(_os.environ.get("MK_E", "4"))
            g = E if en == "gpsimd" else None
            q = E if en == "sync" else None
            v = E if en == "vector" else None
            a = E if en == "scalar" else None
            t = E if en == "tensor" else None

            base = dict(ctx.c)
            for dc in range(NDCR):
                if g and dc >= 2:
                    g.wait_ge(ctx.h["pUD"], base["pUD"] + (dc - 1) * 4)
                ctx.inc(g and g.dma_start(out=wrUD[dc % 2][:],
                                          in_=WINP[dc, :, 0:1024]), "g", 16)
                ctx.mark("g", ("wud", h, dc))
                for L in range(NLC):
                    lg = dc * NLC + L
                    if g and lg >= 4:
                        g.wait_ge(ctx.h["pC"], base["pC"] + (lg - 3) * 4)
                    ctx.inc(g and g.dma_start(
                        out=wrL[lg % 4][:],
                        in_=WINP[dc, :, 1024 + L * 2048: 1024 + (L + 1) * 2048]),
                        "g", 16)
                    ctx.mark("g", ("wL", h, lg))

            for dc in range(NDCR):
                if q:
                    q.wait_ge(ctx.h["aM"], base["aM"] + dc * 8 + 7)
                ctx.inc(q and q.dma_start(out=u2[64:128, :], in_=u2[0:64, :]),
                        "q", 16)
                ctx.mark("q", ("dup", h, dc))
                if DBG == "ud" and dc == 0 and h == 0:
                    if g:
                        ctx.wait(g, "q", ("dup", h, dc))
                        g.wait_ge(ctx.h["vM"], base["vM"] + 4)
                    ctx.inc(g and g.dma_start(out=OUT[0, 0:128, :],
                                              in_=u2[:, 0:1024]), "g", 16)
                    ctx.inc(g and g.dma_start(out=OUT[0, 128:256, :],
                                              in_=u2[:, 1024:2048]), "g", 16)
                    ctx.inc(g and g.dma_start(out=OUT[0, 256:384, :],
                                              in_=d2[:, 0:1024]), "g", 16)
                    ctx.inc(g and g.dma_start(out=OUT[0, 384:512, :],
                                              in_=d2[:, 1024:2048]), "g", 16)
                    ctx.mark("g", ("dbgud",))
                    if v:
                        ctx.wait(v, "g", ("dbgud",))
                if "e" not in SKIP and ELVL >= 4:
                    if q:
                        q.wait_ge(ctx.h["vT"], base["vT"] + dc + 1)
                    ctx.inc(q and q.dma_start(
                        out=ydram[:, :, dc * 64:(dc + 1) * 64].rearrange("s p f -> p s f"),
                        in_=ytstg[:].rearrange("p (s f) -> p s f", f=64)), "q", 16)
                    ctx.mark("q", ("ysp", h, dc))

            pend = []

            def emit_nsum(gp):
                dcp, rem = divmod(gp, NLC * NNT)
                Lp, ntp = divmod(rem, NNT)
                if t:
                    ctx.wait(t, "vC", ("ch", gp))
                    if Lp == 0:
                        ctx.wait(t, "vY", ("ysb", dcp - 1, 3))
                        ctx.wait(t, "vT", ("yev", h, dcp - 1))
                bank = pY[ntp // 2]
                hi = ntp % 2
                mm = t and t.matmul(
                    bank[64 * hi:64 * hi + 64, :],
                    ssels[:, Lp * 128 + 64 * hi: Lp * 128 + 64 * hi + 64],
                    chsb[gp % 2], start=(Lp == 0), stop=(Lp == NLC - 1),
                    skip_group_check=True)
                ctx.inc(mm, "pY", 1)
                ctx.mark("pY", ("nsum", gp))

            for dc in range(NDCR):
                for nt in range(0 if "u" in SKIP else NNT):
                    bank = pB[nt % 2]
                    nsl = slice(nt * 512, (nt + 1) * 512)
                    if t:
                        ctx.wait(t, "g", ("wud", h, dc))
                        ctx.wait(t, "vB", ("budc", dc - 1))
                        ctx.wait(t, "aM", ("spl_a", h, dc - 1, 3))
                        ctx.wait(t, "vM", ("spl", h, dc - 1, 3))
                        if nt >= 2:
                            ctx.wait(t, "aM", ("spl_a", h, dc, nt - 2))
                            ctx.wait(t, "vM", ("spl", h, dc, nt - 2))
                        last = None
                        for k in range(8):
                            last = t.matmul(bank[:],
                                            wrUD[dc % 2][:, k * 128:(k + 1) * 128],
                                            XN1(k)[:, nsl],
                                            start=(k == 0), stop=(k == 7),
                                            skip_group_check=True)
                        ctx.inc(last, "pUD", 1)
                    else:
                        ctx.c["pUD"] += 1
                    ctx.mark("pUD", ("udmm1", h, dc, nt))
                    if nt == 3:
                        ctx.mark("pUD", ("udmm", h, dc))
                    UDLVL = int(_os.environ.get("MK_UD", "3"))
                    if a and UDLVL >= 2:
                        ctx.wait(a, "pUD", ("udmm1", h, dc, nt))
                        ctx.wait(a, "q", ("dup", h, dc - 1))
                        ctx.wait(a, "vY", ("ysb", dc - 1, 3))
                        ctx.wait(a, "pU", ("pu", gidx(dc - 1, NLC - 1, 3)))
                        ctx.inc(a.activation(u2[0:64, nsl], bank[0:64, :], AF.Identity,
                                             bias=buds[0:64, dc:dc + 1]), "aM", 1)
                        ctx.wait(a, "vM", ("spl", h, dc, nt - 1))
                        a.activation(spa[64:128, :], bank[64:128, :], AF.Abs,
                                     bias=buds[64:128, dc:dc + 1])
                        a.activation(spa[64:128, :], spa[64:128, :], AF.Exp,
                                     scale=csc(C_NEG1)[64:128, :])
                        ctx.inc(a.activation(spa[64:128, :], spa[64:128, :], AF.Ln,
                                             bias=csc(C_ONE)[64:128, :]), "aM", 1)
                    elif UDLVL >= 2:
                        ctx.c["aM"] += 2
                    ctx.mark("aM", ("uev", h, dc, nt))
                    ctx.mark("aM", ("spl_a", h, dc, nt))
                    if v and UDLVL >= 3:
                        ctx.wait(v, "pUD", ("udmm1", h, dc, nt))
                        ctx.wait(v, "pD", ("pd", gidx(dc - 1, NLC - 1, nt)))
                        ctx.wait(v, "aM", ("spl_a", h, dc, nt))
                        v.tensor_scalar(spr[64:128, :], bank[64:128, :],
                                        buds[64:128, dc:dc + 1], 0.0,
                                        ALU.add, ALU.max)
                        ctx.inc(v.tensor_tensor(d2[64:128, nsl], spr[64:128, :],
                                                spa[64:128, :], ALU.add), "vM", 1)
                    elif UDLVL >= 3:
                        ctx.c["vM"] += 1
                    ctx.mark("vM", ("spl", h, dc, nt))

                for L in range(0 if "g" in SKIP else NLC):
                    lg = dc * NLC + L
                    for nt in range(NNT):
                        gp = gidx(dc, L, nt)
                        nsl = slice(nt * 512, (nt + 1) * 512)
                        dcl = dc * 8 + L
                        if pend and GLVL >= 6:
                            emit_nsum(pend.pop(0))
                        elif pend and GLVL < 6:
                            pend.pop(0)
                        if t:
                            ctx.wait(t, "g", ("wL", h, lg))
                            ctx.wait(t, "vB", ("bu", gp - 2))
                            if L == 0:
                                ctx.wait(t, "aM", ("spl_a", h, dc, 2 + (nt % 2)))
                                ctx.wait(t, "vM", ("spl", h, dc, 2 + (nt % 2)))
                            last = None
                            for k in range(8):
                                last = t.matmul(pB[gp % 2][:],
                                                wrL[lg % 4][:, k * 128:(k + 1) * 128],
                                                XN1(k)[:, nsl],
                                                start=(k == 0), stop=(k == 7),
                                                skip_group_check=True)
                            ctx.inc(last, "pB", 1)
                        else:
                            ctx.c["pB"] += 1
                        ctx.mark("pB", ("pb", gp))
                        if t:
                            ctx.wait(t, "vC", ("ch", gp - 2))
                            last = None
                            for k in range(8):
                                last = t.matmul(
                                    pC[gp % 2][:],
                                    wrL[lg % 4][:, 1024 + k * 128: 1024 + (k + 1) * 128],
                                    XN1(k)[:, nsl],
                                    start=(k == 0), stop=(k == 7),
                                    skip_group_check=True)
                            ctx.inc(last, "pC", 1)
                        else:
                            ctx.c["pC"] += 1
                        ctx.mark("pC", ("pc", gp))
                        if nt == 3:
                            ctx.mark("pC", ("pcL", h, lg))
                        if t:
                            ctx.wait(t, "aA", ("ab", gp - 1))
                            ctx.wait(t, "vM", ("spl", h, dc, nt))
                            ctx.inc(t.matmul(pD[:],
                                             esels[64:128, L * 128:(L + 1) * 128],
                                             d2[64:128, nsl], start=True, stop=True,
                                             skip_group_check=True), "pD", 1)
                        else:
                            ctx.c["pD"] += 1
                        ctx.mark("pD", ("pd", gp))
                        if t:
                            ctx.wait(t, "aU", ("ue", gp - 1))
                            ctx.wait(t, "aM", ("uev", h, dc, nt))
                            ctx.inc(t.matmul(pS[:],
                                             esels[0:64, L * 128:(L + 1) * 128],
                                             u2[0:64, nsl], start=True, stop=True,
                                             skip_group_check=True), "pU", 1)
                        else:
                            ctx.c["pU"] += 1
                        ctx.mark("pU", ("pu", gp))
                        pend.append(gp)
                        if a and GLVL >= 2:
                            ctx.wait(a, "pD", ("pd", gp))
                            ctx.wait(a, "vS", ("sc", gp - 2))
                            ctx.inc(a.activation(mabar[gp % 2][:], pD[:], AF.Exp,
                                                 scale=acols[:, dcl:dcl + 1]),
                                    "aA", 1)
                            ctx.wait(a, "pU", ("pu", gp))
                            ctx.wait(a, "vB", ("bu", gp - 2))
                            ctx.inc(a.activation(mubc[gp % 2][:], pS[:], AF.Copy),
                                    "aU", 1)
                        elif GLVL >= 2:
                            ctx.c["aA"] += 1
                            ctx.c["aU"] += 1
                        ctx.mark("aA", ("ab", gp))
                        ctx.mark("aU", ("ue", gp))
                        if v and GLVL >= 3:
                            ctx.wait(v, "pB", ("pb", gp))
                            ctx.wait(v, "aU", ("ue", gp))
                            ctx.inc(v.scalar_tensor_tensor(
                                busb[:], pB[gp % 2][:], bBCs[:, dcl:dcl + 1],
                                mubc[gp % 2][:], ALU.add, ALU.mult), "vB", 1)
                        elif GLVL >= 3:
                            ctx.c["vB"] += 1
                        ctx.mark("vB", ("bu", gp))
                        if nt == 3 and L == NLC - 1:
                            ctx.mark("vB", ("budc", dc))
                        if v and GLVL >= 4:
                            ctx.wait(v, "aA", ("ab", gp))
                            ctx.wait(v, "vS", val=ctx.c["vS"])
                            if nt in (0, 2):
                                hst = hstA if nt == 0 else hstB
                                init = 0.0 if h == 0 else hst[:, dcl:dcl + 1]
                            else:
                                init = mh[(gp - 1) % 2][:, 511:512]
                            ctx.inc(v.tensor_tensor_scan(
                                mh[gp % 2][:], mabar[gp % 2][:], busb[:],
                                init, ALU.mult, ALU.add), "vS", 1)
                        elif GLVL >= 4:
                            ctx.c["vS"] += 1
                        ctx.mark("vS", ("sc", gp))
                        if nt in (1, 3):
                            if v and GLVL >= 4:
                                ctx.wait(v, "vS", val=ctx.c["vS"])
                                hst = hstA if nt == 1 else hstB
                                ctx.inc(v.tensor_copy(hst[:, dcl:dcl + 1],
                                                      mh[gp % 2][:, 511:512]),
                                        "vM", 1)
                            elif GLVL >= 4:
                                ctx.c["vM"] += 1
                        if v and GLVL >= 5:
                            ctx.wait(v, "pC", ("pc", gp))
                            ctx.wait(v, "vS", val=ctx.c["vS"])
                            ctx.wait(v, "pY", ("nsum", gp - 2))
                            ctx.inc(v.scalar_tensor_tensor(
                                chsb[gp % 2], pC[gp % 2][:],
                                bBCs[:, 128 + dcl:129 + dcl], mh[gp % 2][:],
                                ALU.add, ALU.mult), "vC", 1)
                        elif GLVL >= 5:
                            ctx.c["vC"] += 1
                        ctx.mark("vC", ("ch", gp))

                for nt in range(0 if "e" in SKIP else NNT):
                    gp_last = gidx(dc, NLC - 1, nt)
                    hi = nt % 2
                    rsl = slice(64 * hi, 64 * hi + 64)
                    nsl = slice(nt * 512, (nt + 1) * 512)
                    if pend and pend[0] <= gp_last:
                        while pend and pend[0] <= gp_last:
                            emit_nsum(pend.pop(0))
                    if v:
                        ctx.wait(v, "pY", ("nsum", gp_last))
                        if hi == 1:
                            ctx.wait(v, "q", ("dup", h, dc))
                        ctx.wait(v, "pT", ("ytr", dc - 1))
                        ctx.inc(v.scalar_tensor_tensor(
                            ysb[rsl, nsl], u2[rsl, nsl], dcols[rsl, dc:dc + 1],
                            pY[nt // 2][rsl, :], ALU.mult, ALU.add), "vY", 1)
                    else:
                        ctx.c["vY"] += 1
                    ctx.mark("vY", ("ysb", dc, nt))
                if t and "e" not in SKIP and ELVL >= 2:
                    last = None
                    for st in range(16):
                        ntq = st // 4
                        qq = st % 4
                        hi = ntq % 2
                        if st == 0:
                            ctx.wait(t, "vY", ("ysb", dc, 3))
                            ctx.wait(t, "vT", ("yev", h, dc - 1))
                        bank = pY[st // 8]
                        last = t.transpose(
                            bank[0:128, (st % 8) * 64:(st % 8) * 64 + 64],
                            ysb[:, ntq * 512 + qq * 128: ntq * 512 + (qq + 1) * 128],
                            idf[:, 64 * hi:64 * hi + 64])
                    ctx.inc(last, "pT", 1)
                elif "e" not in SKIP and ELVL >= 2:
                    ctx.c["pT"] += 1
                ctx.mark("pT", ("ytr", dc))
                if v and "e" not in SKIP and ELVL >= 3:
                    ctx.wait(v, "pT", ("ytr", dc))
                    ctx.wait(v, "q", ("ysp", h, dc - 1))
                    last = None
                    for st in range(16):
                        bank = pY[st // 8]
                        last = v.tensor_copy(ytstg[:, st * 64:(st + 1) * 64],
                                             bank[:, (st % 8) * 64:(st % 8) * 64 + 64])
                    ctx.inc(last, "vT", 1)
                elif "e" not in SKIP and ELVL >= 3:
                    ctx.c["vT"] += 1
                ctx.mark("vT", ("yev", h, dc))
            while pend:
                emit_nsum(pend.pop(0))

        def ph_post(ctx, en, E, h):
            import os as _os
            DBG = _os.environ.get("MK_DBG", "")
            g = E if en == "gpsimd" else None
            q = E if en == "sync" else None
            v = E if en == "vector" else None
            a = E if en == "scalar" else None
            t = E if en == "tensor" else None
            XO = XN2 if h == 0 else XN1

            for tt in range(NTILE):
                if g:
                    ctx.wait(g, "vM", ("px", h, tt - 2))
                ctx.inc(g and g.dma_start(out=xstg[tt % 2][:],
                                          in_=x_tile_ap(h, tt)), "g", 16)
                ctx.mark("g", ("pxd", h, tt))
                if q:
                    ctx.wait(q, "vM", ("py", h, tt - 2))
                ctx.inc(q and q.dma_start(out=ystg[tt % 2][:], in_=ydram[tt]),
                        "q", 16)
                ctx.mark("q", ("pyd", h, tt))
                if DBG == "y":
                    if g:
                        ctx.wait(g, "q", ("pyd", h, tt))
                    ctx.inc(g and g.dma_start(out=out_tile_ap(h, tt),
                                              in_=ystg[tt % 2][:]), "g", 16)
                if v:
                    ctx.wait(v, "q", ("pyd", h, tt))
                    ctx.wait(v, "aM", ("s2", h, tt - 1))
                ln_stats(v, ystg[tt % 2][:] if v else None, rc(R_MY))
                ctx.mark("vM", ("sty", h, tt))
                if a:
                    ctx.wait(a, "vM", ("sty", h, tt))
                    a.activation(rc(R_TMP), rc(R_VAR), AF.Ln, bias=csc(C_EPS))
                    ctx.inc(a.activation(rc(R_RY), rc(R_TMP), AF.Exp,
                                         scale=csc(C_NHALF)), "aM", 1)
                else:
                    ctx.c["aM"] += 1
                ctx.mark("aM", ("ry", h, tt))
                if v:
                    ctx.wait(v, "g", ("pxd", h, tt))
                    ctx.wait(v, "aM", ("ry", h, tt))
                    ctx.wait(v, "pT", ("ptr", h, tt - 2))
                    v.tensor_scalar(x1f[tt % 2][:], ystg[tt % 2][:], rc(R_MY),
                                    rc(R_RY), ALU.subtract, ALU.mult)
                    v.tensor_tensor(x1f[tt % 2][:], x1f[tt % 2][:],
                                    xstg[tt % 2][:], ALU.add)
                ln_stats(v, x1f[tt % 2][:] if v else None,
                         m2s[:, h * 16 + tt: h * 16 + tt + 1])
                ctx.mark("vM", ("st2", h, tt))
                ctx.mark("vM", ("py", h, tt))
                ctx.mark("vM", ("px", h, tt))
                if DBG == "x1":
                    if q:
                        ctx.wait(q, "vM", ("st2", h, tt))
                    ctx.inc(q and q.dma_start(out=out_tile_ap(h, tt),
                                              in_=x1f[tt % 2][:]), "q", 16)
                    ctx.mark("q", ("dbgx1", h, tt))
                if v and DBG == "x1":
                    ctx.wait(v, "q", ("dbgx1", h, tt))
                if a:
                    ctx.wait(a, "vM", ("st2", h, tt))
                    a.activation(rc(R_TMP), rc(R_VAR), AF.Ln, bias=csc(C_EPS))
                    a.activation(s2s[:, h * 16 + tt: h * 16 + tt + 1], rc(R_TMP),
                                 AF.Exp, scale=csc(C_PHALF))
                    ctx.inc(a.activation(rc(R_R1), rc(R_TMP), AF.Exp,
                                         scale=csc(C_NHALF)), "aM", 1)
                else:
                    ctx.c["aM"] += 1
                ctx.mark("aM", ("s2", h, tt))
                if v:
                    ctx.wait(v, "aM", ("s2", h, tt))
                    ctx.inc(v.tensor_scalar(
                        x1f[tt % 2][:], x1f[tt % 2][:],
                        m2s[:, h * 16 + tt: h * 16 + tt + 1], rc(R_R1),
                        ALU.subtract, ALU.mult), "vM", 1)
                else:
                    ctx.c["vM"] += 1
                ctx.mark("vM", ("xn2", h, tt))
                if t:
                    ctx.wait(t, "vM", ("xn2", h, tt))
                    ctx.wait(t, "vT", ("pev", h, tt - 1))
                    last = None
                    for jj in range(8):
                        bank = pY[jj // 4]
                        last = t.transpose(bank[:, (jj % 4) * 128:(jj % 4 + 1) * 128],
                                           x1f[tt % 2][:, jj * 128:(jj + 1) * 128],
                                           idf[:])
                    ctx.inc(last, "pT", 1)
                else:
                    ctx.c["pT"] += 1
                ctx.mark("pT", ("ptr", h, tt))
                if v:
                    ctx.wait(v, "pT", ("ptr", h, tt))
                    last = None
                    for jj in range(8):
                        bank = pY[jj // 4]
                        last = v.tensor_copy(XO(jj)[:, tt * 128:(tt + 1) * 128],
                                             bank[:, (jj % 4) * 128:(jj % 4 + 1) * 128])
                    ctx.inc(last, "vT", 1)
                else:
                    ctx.c["vT"] += 1
                ctx.mark("vT", ("pev", h, tt))
                if t:
                    ctx.wait(t, "vT", ("pev", h, tt))
                    ctx.wait(t, "vM", ("lgev", h, tt - 1))
                    last = None
                    for k in range(8):
                        last = t.matmul(pS[:, 0:8],
                                        XO(k)[:, tt * 128:(tt + 1) * 128],
                                        gws[:, k * 8:(k + 1) * 8],
                                        start=(k == 0), stop=(k == 7),
                                        skip_group_check=True)
                    ctx.inc(last, "pG", 1)
                else:
                    ctx.c["pG"] += 1
                ctx.mark("pG", ("lg", h, tt))
                if v:
                    ctx.wait(v, "pG", ("lg", h, tt))
                    ctx.inc(v.tensor_copy(
                        lgs[:, (h * 16 + tt) * 8:(h * 16 + tt + 1) * 8],
                        pS[:, 0:8]), "vM", 1)
                else:
                    ctx.c["vM"] += 1
                ctx.mark("vM", ("lgev", h, tt))
            if h == 1:
                for k in range(8):
                    if q:
                        ctx.wait(q, "vT", ("pev", h, NTILE - 1))
                    ctx.inc(q and q.dma_start(out=xn2sp[k], in_=XO(k)), "q", 16)

        def ph_gate(ctx, en, E):
            import os as _os
            DBG = _os.environ.get("MK_DBG", "")
            g = E if en == "gpsimd" else None
            q = E if en == "sync" else None
            v = E if en == "vector" else None
            a = E if en == "scalar" else None
            t = E if en == "tensor" else None
            ctx.inc(g and g.dma_start(out=gate9[8:9, :], in_=ONES4096), "g", 16)
            ctx.mark("g", ("g9one",))
            for i in range(32):
                sl = slice(i * 8, (i + 1) * 8)
                if v:
                    ctx.wait(v, "aX", ("sm2", i - 1))
                    vg(v, v.reduce_max(rc(R_TMP), lgs[:, sl], AX.X), "vX")
                    vg(v, v.tensor_single_scalar(rc(R_TMP), rc(R_TMP), -1.0,
                                                 ALU.mult), "vX")
                else:
                    ctx.c["vX"] += 2
                ctx.mark("vX", ("sm1", i))
                if a:
                    ctx.wait(a, "vX", ("sm1", i))
                    ctx.inc(a.activation(sm_e, lgs[:, sl], AF.Exp,
                                         bias=rc(R_TMP)), "aX", 1)
                else:
                    ctx.c["aX"] += 1
                ctx.mark("aX", ("sm2", i))
                if v:
                    ctx.wait(v, "aX", ("sm2", i))
                    vg(v, v.reduce_sum(rc(R_VAR), sm_e, AX.X), "vX")
                    vg(v, v.reciprocal(rc(R_VAR), rc(R_VAR)), "vX")
                    vg(v, v.tensor_single_scalar(lgs[:, sl], sm_e,
                                                 rc(R_VAR), ALU.mult), "vX")
                else:
                    ctx.c["vX"] += 3
                ctx.mark("vX", ("sm3", i))
            if q:
                ctx.wait(q, "vX", ("sm3", 31))
            ctx.inc(q and q.dma_start(out=cc_in_t.ap(), in_=lgs[:]), "q", 16)
            ctx.mark("q", ("ccin",))
            if g:
                ctx.wait(g, "q", ("ccin",))
                ctx.inc(g.collective_compute(
                    "AllGather", ALU.bypass, replica_groups=[list(range(8))],
                    ins=[cc_in_t.ap().opt()], outs=[cc_out_t.ap().opt()]), "cc", 1)
                ctx.wait(g, "cc", val=ctx.c["cc"])
                ctx.inc(g.dma_start(
                    out=gath, in_=cc_out_t.ap().rearrange("(r p) f -> p r f", r=8)),
                    "g", 16)
            else:
                ctx.c["cc"] += 1
                ctx.c["g"] += 16
            ctx.mark("g", ("gath",))
            if v:
                ctx.wait(v, "g", ("gath",))
                v.memset(b_lo, 0.0)
                v.memset(b_hi, 1.0)
                v.memset(b_tau, 0.5)
                ctx.inc(v.memset(b_tcol, 0.5), "vX", 1)
            else:
                ctx.c["vX"] += 1
            for r in range(NBIS):
                if v:
                    ctx.wait(v, "vX", val=ctx.c["vX"])
                    vg(v, v.tensor_single_scalar(AF32[:, 2048:4096], gath, b_tcol,
                                                 ALU.is_gt), "vX")
                    vg(v, v.reduce_sum(b_cnt, AF32[:, 2048:4096], AX.X), "vX")
                else:
                    ctx.c["vX"] += 2
                ctx.mark("vX", ("cnt", r))
                if t:
                    ctx.wait(t, "vX", ("cnt", r))
                    ctx.inc(t.matmul(pS[0:1, 16:17], b_cnt, ones128[:, 0:1],
                                     start=True, stop=True,
                                     skip_group_check=True), "pX", 1)
                else:
                    ctx.c["pX"] += 1
                ctx.mark("pX", ("tot", r))
                if a:
                    ctx.wait(a, "pX", ("tot", r))
                    ctx.inc(a.copy(b_tot, pS[0:1, 16:17]), "aX", 1)
                else:
                    ctx.c["aX"] += 1
                ctx.mark("aX", ("tot", r))
                if v:
                    ctx.wait(v, "aX", ("tot", r))
                    vg(v, v.tensor_single_scalar(b_m, b_tot, KTOP, ALU.is_ge), "vX")
                    vg(v, v.tensor_tensor(b_t1, b_tau, b_lo, ALU.subtract), "vX")
                    vg(v, v.tensor_tensor(b_t2, b_m, b_t1, ALU.mult), "vX")
                    vg(v, v.tensor_tensor(b_lo, b_lo, b_t2, ALU.add), "vX")
                    vg(v, v.tensor_tensor(b_t1, b_hi, b_tau, ALU.subtract), "vX")
                    vg(v, v.tensor_scalar(b_m, b_m, -1.0, 1.0, ALU.mult, ALU.add), "vX")
                    vg(v, v.tensor_tensor(b_t2, b_m, b_t1, ALU.mult), "vX")
                    vg(v, v.tensor_tensor(b_hi, b_hi, b_t2, ALU.subtract), "vX")
                    vg(v, v.tensor_tensor(b_t1, b_lo, b_hi, ALU.add), "vX")
                    vg(v, v.tensor_single_scalar(b_tau, b_t1, 0.5, ALU.mult), "vX")
                else:
                    ctx.c["vX"] += 10
                ctx.mark("vX", ("upd", r))
                if t:
                    ctx.wait(t, "vX", ("upd", r))
                    src = b_lo if r == NBIS - 1 else b_tau
                    ctx.inc(t.matmul(pS[:, 17:18], ones128[0:1, 0:128], src,
                                     start=True, stop=True,
                                     skip_group_check=True), "pX", 1)
                else:
                    ctx.c["pX"] += 1
                ctx.mark("pX", ("bc", r))
                if v:
                    ctx.wait(v, "pX", ("bc", r))
                    ctx.inc(v.tensor_copy(b_tcol, pS[:, 17:18]), "vX", 1)
                else:
                    ctx.c["vX"] += 1
            for i in range(32):
                sl = slice(i * 8, (i + 1) * 8)
                if v:
                    ctx.wait(v, "pX", ("gtr", i - 1))
                    vg(v, v.tensor_single_scalar(sm_e, lgs[:, sl], b_tcol,
                                                 ALU.is_gt), "vX")
                    vg(v, v.tensor_tensor(sm_g, sm_e, lgs[:, sl], ALU.mult), "vX")
                else:
                    ctx.c["vX"] += 2
                ctx.mark("vX", ("gm", i))
                if t:
                    ctx.wait(t, "vX", ("gm", i))
                    ctx.inc(t.transpose(pS[0:8, 32:160], sm_g, idf[:]), "pX", 1)
                else:
                    ctx.c["pX"] += 1
                ctx.mark("pX", ("gtr", i))
                if v:
                    ctx.wait(v, "pX", ("gtr", i))
                    ctx.wait(v, "g", ("g9one",))
                    ctx.inc(v.tensor_copy(gate9[0:8, i * 128:(i + 1) * 128],
                                          pS[0:8, 32:160]), "vX", 1)
                else:
                    ctx.c["vX"] += 1
                ctx.mark("vX", ("g9", i))
            if DBG == "gate":
                for qq in range(4):
                    if g:
                        ctx.wait(g, "vX", ("g9", 31))
                    ctx.inc(g and g.dma_start(
                        out=OUT[0, qq * 9:(qq + 1) * 9, :],
                        in_=gate9[:, qq * 1024:(qq + 1) * 1024]), "g", 16)
                ctx.mark("g", ("dbgg",))
                if v:
                    ctx.wait(v, "g", ("dbgg",))

        def ph_ffn(ctx, en, E, h):
            import os as _os
            DBG = _os.environ.get("MK_DBG", "")
            g = E if en == "gpsimd" else None
            q = E if en == "sync" else None
            v = E if en == "vector" else None
            a = E if en == "scalar" else None
            t = E if en == "tensor" else None
            aG0 = ctx.c["aG"]
            vA0 = ctx.c["vA"]
            ctx_w1base = ctx.c["pW1"]
            ctx_w2base = ctx.c["pW2"]

            if h == 1:
                for k in range(8):
                    ctx.inc(q and q.dma_start(out=XN2(k), in_=xn2sp[k]), "q", 16)
                ctx.mark("q", ("rld",))

            for ei in range(NEI):
                for fb in range(NFB):
                    gi = ei * NFB + fb
                    for j in range(8):
                        jseq = gi * 8 + j
                        if g and jseq >= 4:
                            g.wait_ge(ctx.h["pW1"], aG0 * 0 + ctx_w1base + (jseq - 3) * 4)
                        ctx.inc(g and g.dma_start(
                            out=w1r[jseq % 4][:],
                            in_=WFFN[gi, :, j * 1024:(j + 1) * 1024]), "g", 16)
                        ctx.mark("g", ("w1", gi, j))
                    for m in range(8):
                        mseq = gi * 8 + m
                        if g and mseq >= 4:
                            g.wait_ge(ctx.h["pW2"], ctx_w2base + (mseq - 3) * 4)
                        ctx.inc(g and g.dma_start(
                            out=w2r[mseq % 4][:],
                            in_=WFFN[gi, :, 8192 + m * 1024: 8192 + (m + 1) * 1024]),
                            "g", 16)
                        ctx.mark("g", ("w2", gi, m))

            w1seq = 0
            w2seq = 0
            for ei in range(NEI):
                if ei > 0:
                    for qq in range(4):
                        if t:
                            ctx.wait(t, "vM", ("gbcev", ei - 1, 3))
                            ctx.wait(t, "vM", ("gbcev", ei, qq - 1))
                            ctx.inc(t.matmul(
                                pD[:], e9s[0:9, (ei - 1) * 128: ei * 128],
                                gate9[0:9, h * 2048 + qq * 512:
                                      h * 2048 + (qq + 1) * 512],
                                start=True, stop=True,
                                skip_group_check=True), "pG", 1)
                        else:
                            ctx.c["pG"] += 1
                        ctx.mark("pG", ("gbc", ei, qq))
                        if v:
                            ctx.wait(v, "pG", ("gbc", ei, qq))
                            ctx.inc(v.tensor_copy(gbc[:, qq * 512:(qq + 1) * 512],
                                                  pD[:]), "vM", 1)
                        else:
                            ctx.c["vM"] += 1
                        ctx.mark("vM", ("gbcev", ei, qq))
                for fb in range(NFB):
                    gi = ei * NFB + fb
                    for j in range(8):
                        jseq = gi * 8 + j
                        bcol = b1cs[:, ei * 32 + fb * 8 + j: ei * 32 + fb * 8 + j + 1]
                        for nt in range(NNT):
                            gsl = slice(nt * 512, (nt + 1) * 512)
                            bank = pB[w1seq % 2]
                            if t:
                                ctx.wait(t, "g", ("w1", gi, j))
                                if h == 1 and gi == 0 and j == 0 and nt == 0:
                                    ctx.wait(t, "q", ("rld",))
                                ctx.wait(t, "aG", val=max(aG0, aG0 + w1seq - 1))
                                last = None
                                for k in range(8):
                                    last = t.matmul(
                                        bank[:],
                                        w1r[jseq % 4][:, k * 128:(k + 1) * 128],
                                        XN2(k)[:, gsl],
                                        start=(k == 0), stop=(k == 7),
                                        skip_group_check=True)
                                ctx.inc(last, "pW1", 1)
                            else:
                                ctx.c["pW1"] += 1
                            ctx.mark("pW1", ("w1g", gi, j, nt))
                            if nt == 3:
                                ctx.mark("pW1", ("w1f", jseq))
                            if a:
                                ctx.wait(a, "pW1", ("w1g", gi, j, nt))
                                ctx.wait(a, "vH", ("hp", jseq - 2))
                                a.activation(xbV[j % 2][:, gsl], bank[:], AF.Identity,
                                             bias=bcol)
                                ctx.wait(a, "vZ", ("p4", jseq - 2))
                                ctx.inc(a.activation(sqV[j % 2][:, gsl], bank[:],
                                                     AF.Square, bias=bcol), "aG", 1)
                            else:
                                ctx.c["aG"] += 1
                            w1seq += 1
                        if v:
                            ctx.wait(v, "aG", val=aG0 + w1seq)
                            v.tensor_scalar(sqV[j % 2][:], sqV[j % 2][:], GC_A, 1.0,
                                            ALU.mult, ALU.add)
                            ctx.inc(v.tensor_tensor(sqV[j % 2][:], sqV[j % 2][:],
                                                    xbV[j % 2][:], ALU.mult), "vZ", 1)
                        else:
                            ctx.c["vZ"] += 1
                        ctx.mark("vZ", ("p4", jseq))
                        if a:
                            ctx.wait(a, "vZ", ("p4", jseq))
                            ctx.inc(a.activation(sqV[j % 2][:], sqV[j % 2][:],
                                                 AF.Sigmoid, scale=csc(C_GS)),
                                    "aS", 1)
                        else:
                            ctx.c["aS"] += 1
                        ctx.mark("aS", ("p5", jseq))
                        if v:
                            ctx.wait(v, "aS", ("p5", jseq))
                            ctx.wait(v, "pW2", ("w2last", gi - 1))
                            if ei == 0:
                                ctx.inc(v.tensor_tensor(HBUF(j), sqV[j % 2][:],
                                                        xbV[j % 2][:], ALU.mult),
                                        "vH", 1)
                            else:
                                v.tensor_tensor(sqV[j % 2][:], sqV[j % 2][:],
                                                xbV[j % 2][:], ALU.mult)
                                ctx.wait(v, "vM", ("gbcev", ei, 3))
                                ctx.inc(v.tensor_tensor(HBUF(j), sqV[j % 2][:],
                                                        gbc[:], ALU.mult), "vH", 1)
                        else:
                            ctx.c["vH"] += 1
                        ctx.mark("vH", ("hp", jseq))
                        ctx.mark("vH", ("hpk", gi, j))
                    for m in range(8):
                        mseq = gi * 8 + m
                        for nt in range(NNT):
                            gsl = slice(nt * 512, (nt + 1) * 512)
                            bank = pC[w2seq % 2]
                            if t:
                                ctx.wait(t, "g", ("w2", gi, m))
                                ctx.wait(t, "vA", val=max(vA0, vA0 + w2seq - 1))
                                if m == 0 and nt == 0:
                                    for k in range(8):
                                        ctx.wait(t, "vH", ("hpk", gi, k))
                                else:
                                    ctx.wait(t, "vH", ("hpk", gi, 7))
                                nk9 = (ei == NEI - 1 and fb == NFB - 1)
                                last = None
                                for k in range(8):
                                    last = t.matmul(
                                        bank[:],
                                        w2r[mseq % 4][:, k * 128:(k + 1) * 128],
                                        HBUF(k)[:, gsl],
                                        start=(k == 0),
                                        stop=(k == 7 and not nk9),
                                        skip_group_check=True)
                                if nk9:
                                    last = t.matmul(
                                        bank[:], b2s[0:9, m * 128:(m + 1) * 128],
                                        gate9[0:9, h * 2048 + nt * 512:
                                              h * 2048 + (nt + 1) * 512],
                                        start=False, stop=True,
                                        skip_group_check=True)
                                ctx.inc(last, "pW2", 1)
                            else:
                                ctx.c["pW2"] += 1
                            ctx.mark("pW2", ("w2g", gi, m, nt))
                            if nt == 3:
                                ctx.mark("pW2", ("w2f", mseq))
                                if m == 7:
                                    ctx.mark("pW2", ("w2last", gi))
                            if v:
                                ctx.wait(v, "pW2", ("w2g", gi, m, nt))
                                if ei == 0 and fb == 0:
                                    ctx.inc(v.tensor_copy(OUTH(m)[:, gsl], bank[:]),
                                            "vA", 1)
                                else:
                                    ctx.inc(v.tensor_tensor(OUTH(m)[:, gsl], bank[:],
                                                            OUTH(m)[:, gsl],
                                                            ALU.add), "vA", 1)
                            else:
                                ctx.c["vA"] += 1
                            ctx.mark("vA", ("addlast",))
                            w2seq += 1
            if DBG == "moe":
                for m in range(8):
                    if g:
                        ctx.wait(g, "vA", ("addlast",))
                    ctx.inc(g and g.dma_start(
                        out=OUT[0, m * 256:(m + 1) * 256, :].rearrange(
                            "(p c) d -> p (c d)", c=2),
                        in_=OUTH(m)), "g", 16)
                ctx.mark("g", ("dbgmoe",))
                if v:
                    ctx.wait(v, "g", ("dbgmoe",))

        def ph_ffn_tail(ctx, en, E, h):
            pass

        def ph_fin(ctx, en, E, h):
            q = E if en == "sync" else None
            v = E if en == "vector" else None
            a = E if en == "scalar" else None
            t = E if en == "tensor" else None
            for tt in range(NTILE):
                tsl = slice(tt * 128, (tt + 1) * 128)
                if v:
                    ctx.wait(v, "pT", ("ftr2", h, tt - 1))
                    last = None
                    for m in range(8):
                        last = v.tensor_copy(trsrc[:, m * 128:(m + 1) * 128],
                                             XN2(m)[:, tsl])
                    ctx.inc(last, "vM", 1)
                else:
                    ctx.c["vM"] += 1
                ctx.mark("vM", ("fx", h, tt))
                if t:
                    ctx.wait(t, "vM", ("fx", h, tt))
                    ctx.wait(t, "vM", ("foev", h, tt - 1))
                    last = None
                    for m in range(8):
                        bank = pY[m // 4]
                        last = t.transpose(bank[:, (m % 4) * 128:(m % 4 + 1) * 128],
                                           trsrc[:, m * 128:(m + 1) * 128], idf[:])
                    ctx.inc(last, "pT", 1)
                else:
                    ctx.c["pT"] += 1
                ctx.mark("pT", ("ftr1", h, tt))
                if v:
                    ctx.wait(v, "pT", ("ftr1", h, tt))
                    last = None
                    for m in range(8):
                        bank = pY[m // 4]
                        last = v.tensor_scalar(
                            x1fin[:, m * 128:(m + 1) * 128],
                            bank[:, (m % 4) * 128:(m % 4 + 1) * 128],
                            s2s[:, h * 16 + tt: h * 16 + tt + 1],
                            m2s[:, h * 16 + tt: h * 16 + tt + 1],
                            ALU.mult, ALU.add)
                    ctx.inc(last, "vM", 1)
                else:
                    ctx.c["vM"] += 1
                ctx.mark("vM", ("fx1", h, tt))
                if v:
                    last = None
                    for m in range(8):
                        last = v.tensor_copy(trsrc[:, m * 128:(m + 1) * 128],
                                             OUTH(m)[:, tsl])
                    ctx.inc(last, "vM", 1)
                else:
                    ctx.c["vM"] += 1
                ctx.mark("vM", ("fo", h, tt))
                if t:
                    ctx.wait(t, "vM", ("fo", h, tt))
                    ctx.wait(t, "vM", ("fx1", h, tt))
                    last = None
                    for m in range(8):
                        bank = pY[m // 4]
                        last = t.transpose(bank[:, (m % 4) * 128:(m % 4 + 1) * 128],
                                           trsrc[:, m * 128:(m + 1) * 128], idf[:])
                    ctx.inc(last, "pT", 1)
                else:
                    ctx.c["pT"] += 1
                ctx.mark("pT", ("ftr2", h, tt))
                if v:
                    ctx.wait(v, "pT", ("ftr2", h, tt))
                    ctx.wait(v, "q", ("outd", h, tt - 2))
                    ctx.wait(v, "aM", ("fr3", h, tt - 1))
                    last = None
                    for m in range(8):
                        bank = pY[m // 4]
                        last = v.tensor_copy(outtmV[tt % 2][:, m * 128:(m + 1) * 128],
                                             bank[:, (m % 4) * 128:(m % 4 + 1) * 128])
                ln_stats(v, outtmV[tt % 2][:] if v else None, rc(R_M3))
                ctx.mark("vM", ("foev", h, tt))
                ctx.mark("vM", ("fst", h, tt))
                if a:
                    ctx.wait(a, "vM", ("fst", h, tt))
                    a.activation(rc(R_TMP), rc(R_VAR), AF.Ln, bias=csc(C_EPS))
                    ctx.inc(a.activation(rc(R_R3), rc(R_TMP), AF.Exp,
                                         scale=csc(C_NHALF)), "aM", 1)
                else:
                    ctx.c["aM"] += 1
                ctx.mark("aM", ("fr3", h, tt))
                if v:
                    ctx.wait(v, "aM", ("fr3", h, tt))
                    v.tensor_scalar(outtmV[tt % 2][:], outtmV[tt % 2][:],
                                    rc(R_M3), rc(R_R3), ALU.subtract, ALU.mult)
                    ctx.inc(v.tensor_tensor(outtmV[tt % 2][:], outtmV[tt % 2][:],
                                            x1fin[:], ALU.add), "vM", 1)
                else:
                    ctx.c["vM"] += 1
                ctx.mark("vM", ("fdone", h, tt))
                if q:
                    ctx.wait(q, "vM", ("fdone", h, tt))
                ctx.inc(q and q.dma_start(out=out_tile_ap(h, tt),
                                          in_=outtmV[tt % 2][:]), "q", 16)
                ctx.mark("q", ("outd", h, tt))

        def ph_end(ctx, en, E):
            if en == "sync":
                E.wait_ge(ctx.h["q"], ctx.c["q"])
                E.wait_ge(ctx.h["g"], ctx.c["g"])

        def run_phase(fn, *args):
            snap = ctx.snap()
            final = None
            for en in ("gpsimd", "sync", "tensor", "scalar", "vector"):
                ctx.load(snap)

                def body(EE, en=en):
                    for s in SEMS:
                        if snap[s] > 0:
                            EE.wait_ge(ctx.h[s], snap[s])
                    fn(ctx, en, EE, *args)

                getattr(block, en)(body)
                if final is None:
                    final = ctx.snap()
                else:
                    assert final == ctx.snap(), (
                        f"phase {fn.__name__} counter divergence on {en}: "
                        f"{ {k: (final[k], ctx.c[k]) for k in final if final[k] != ctx.c[k]} }")

        import os as _os
        _plim = int(_os.environ.get("MK_PHASES", "99"))
        phases = [(ph_init,)]
        for h in (0, 1):
            phases += [(ph_ln1, h), (ph_mamba, h), (ph_post, h)]
        phases.append((ph_gate,))
        for h in (0, 1):
            phases += [(ph_ffn, h), (ph_fin, h)]
        for ph in phases[:_plim]:
            run_phase(*ph)
        run_phase(ph_end)

    return nc


def kernel(L0_attn_x, L0_attn_W, L0_attn_b, L0_attn_A, L0_attn_D,
           L0_ffn_gate_weight, L0_ffn_W1, L0_ffn_b1, L0_ffn_W2, L0_ffn_b2,
           L0_ffn_Ws1, L0_ffn_bs1, L0_ffn_Ws2, L0_ffn_bs2):
    host = _host_prep(np.asarray(L0_attn_W), np.asarray(L0_attn_b),
                      np.asarray(L0_attn_A), np.asarray(L0_attn_D),
                      np.asarray(L0_ffn_gate_weight), np.asarray(L0_ffn_W1),
                      np.asarray(L0_ffn_b1), np.asarray(L0_ffn_W2),
                      np.asarray(L0_ffn_b2), np.asarray(L0_ffn_Ws1),
                      np.asarray(L0_ffn_bs1), np.asarray(L0_ffn_Ws2),
                      np.asarray(L0_ffn_bs2))
    nc = build_nc()
    x = np.ascontiguousarray(np.asarray(L0_attn_x))
    in_maps = []
    for c in range(8):
        m = dict(host)
        m["x"] = np.ascontiguousarray(x[2 * c: 2 * c + 2])
        in_maps.append(m)
    res = run_bass_kernel_spmd(nc, in_maps, core_ids=list(range(8)))
    return np.concatenate([r["out"] for r in res.results], axis=0)


# version 03 (final)
# speedup vs baseline: 1.0057x; 1.0057x over previous
from contextlib import ExitStack

import numpy as np
import ml_dtypes

import concourse.bass as bass
import concourse.mybir as mybir
from concourse.alu_op_type import AluOpType as ALU
from concourse.bass_utils import run_bass_kernel_spmd

F32 = mybir.dt.float32
BF16 = mybir.dt.bfloat16
AF = mybir.ActivationFunctionType
AX = mybir.AxisListType
BF = ml_dtypes.bfloat16

D = 1024
TSEQ = 2048
BLOC = 2
HT = 1024
HS = 2048
NTILE = 16
NNT = 4
NDC = 16
NLC = 8
NEI = 9
NFB = 4
KTOP = 65536.0
EPS = 1e-5
GC_A = 0.044715
GC_S = 1.5957691216057308
NBIS = 26

SEMS = [
    "g", "q", "cc",
    "gX0", "gX1", "gU0", "gU1", "gL0", "gL1", "gL2", "gL3",
    "gW10", "gW11", "gW12", "gW13", "gW20", "gW21", "gW22", "gW23",
    "qP0", "qP1", "qO0", "qO1", "qD", "qY",
    "pB", "pC", "pD", "pU", "pY", "pUD", "pT", "pG", "pW1", "pW2", "pX",
    "aA", "aU", "aM", "aG", "aS", "aX",
    "vB", "vS", "vC", "vY", "vT", "vM", "vZ", "vH", "vA", "vX",
]


class Ctx:
    def __init__(self, nc, stk):
        self.nc = nc
        self.h = {n: stk.enter_context(nc.semaphore(n)) for n in SEMS}
        self.c = {n: 0 for n in SEMS}
        self.mk = {}

    def snap(self):
        return dict(self.c)

    def load(self, s):
        self.c = dict(s)
        self.mk = {}

    def inc(self, inst, sem, k=1):
        self.c[sem] += k
        if inst is not None:
            inst.then_inc(self.h[sem], k)
        return inst

    def mark(self, sem, key):
        self.mk[(sem,) + key] = self.c[sem]

    def wait(self, E, sem, key=None, val=None):
        if val is None:
            k = (sem,) + key
            if k not in self.mk:
                return
            val = self.mk[k]
        if E is not None and val > 0:
            E.wait_ge(self.h[sem], val)



def _host_prep(W, b, A, Dm, gw, W1, b1, W2, b2, Ws1, bs1, Ws2, bs2):
    WT = np.ascontiguousarray(W.T)
    p = np.arange(128)

    winp = np.empty((NDC, 128, 17408), np.float32)
    for dc in range(NDC):
        cols = []
        rows_ud = np.concatenate([np.arange(dc * 64, dc * 64 + 64),
                                  1024 + np.arange(dc * 64, dc * 64 + 64)])
        cols.append(np.concatenate(list(WT[:, rows_ud].reshape(8, 128, 128)), axis=1))
        for L in range(NLC):
            dvec = dc * 64 + L * 8 + p // 16
            for base in (2048, 18432):
                rows = base + dvec * 16 + p % 16
                cols.append(np.concatenate(list(WT[:, rows].reshape(8, 128, 128)),
                                           axis=1))
        winp[dc] = np.concatenate(cols, axis=1)

    wffn = np.empty((NEI * NFB, 128, 16384), np.float32)
    for ei in range(NEI):
        W1e = Ws1 if ei == 0 else W1[ei - 1]
        W2e = Ws2 if ei == 0 else W2[ei - 1]
        for fb in range(NFB):
            cols = []
            for j in range(8):
                blk = W1e[:, fb * 1024 + j * 128: fb * 1024 + (j + 1) * 128]
                cols.append(np.concatenate(list(blk.reshape(8, 128, 128)), axis=1))
            for m in range(8):
                cols.append(np.concatenate(
                    [W2e[fb * 1024 + k * 128: fb * 1024 + (k + 1) * 128,
                         m * 128:(m + 1) * 128] for k in range(8)], axis=1))
            wffn[ei * NFB + fb] = np.concatenate(cols, axis=1)

    bud = np.empty((128, NDC), np.float32)
    bBC = np.empty((128, 256), np.float32)
    acol = np.empty((128, 128), np.float32)
    dcol = np.empty((128, NDC), np.float32)
    for dc in range(NDC):
        bud[0:64, dc] = b[dc * 64: dc * 64 + 64]
        bud[64:128, dc] = b[1024 + dc * 64: 1024 + dc * 64 + 64]
        dcol[:, dc] = Dm[dc * 64 + (p % 64)]
        for L in range(NLC):
            dvec = dc * 64 + L * 8 + p // 16
            bBC[:, dc * 8 + L] = b[2048 + dvec * 16 + p % 16]
            bBC[:, 128 + dc * 8 + L] = b[18432 + dvec * 16 + p % 16]
            acol[:, dc * 8 + L] = A[dvec, p % 16]

    b1c = np.empty((128, NEI * 32), np.float32)
    for ei in range(NEI):
        b1e = bs1 if ei == 0 else b1[ei - 1]
        for fb in range(NFB):
            for j in range(8):
                b1c[:, ei * 32 + fb * 8 + j] = b1e[fb * 1024 + j * 128:
                                                   fb * 1024 + (j + 1) * 128]

    b2bs2 = np.empty((9, 1024), np.float32)
    b2bs2[0:8] = b2
    b2bs2[8] = bs2

    gwh = np.empty((128, 64), np.float32)
    for k in range(8):
        gwh[:, k * 8:(k + 1) * 8] = gw[:, k * 128:(k + 1) * 128].T

    esel = np.zeros((128, 8 * 128), np.float32)
    ssel = np.zeros((128, 8 * 128), np.float32)
    for L in range(NLC):
        for q in range(128):
            esel[(L * 8 + q // 16) % 64, L * 128 + q] = 1
            esel[64 + (L * 8 + q // 16) % 64, L * 128 + q] = 1
        for pp in range(128):
            m = L * 8 + pp // 16
            ssel[pp, L * 128 + m] = 1
            ssel[pp, L * 128 + 64 + m] = 1
    e9sel = np.zeros((9, 8 * 128), np.float32)
    for e in range(8):
        e9sel[e, e * 128:(e + 1) * 128] = 1
    identf = np.eye(128, dtype=np.float32)
    cst = np.zeros((128, 8), np.float32)
    for i, vv in enumerate([EPS, -0.5, 0.5, GC_A, GC_S, -1.0, 1.0, 0.0]):
        cst[:, i] = vv
    ones4096 = np.ones((1, 4096), np.float32)

    return dict(winp=winp, wffn=wffn, bud=bud, bBC=bBC, acol=acol, dcol=dcol,
                b1c=b1c, b2bs2=b2bs2, gwh=gwh, esel=esel, ssel=ssel,
                e9sel=e9sel, identf=identf, cst=cst, ones4096=ones4096)



def build_nc():
    nc = bass.Bass("TRN2", target_bir_lowering=False)

    X = nc.dram_tensor("x", [BLOC, TSEQ, D], F32, kind="ExternalInput").ap()
    WINP = nc.dram_tensor("winp", [NDC, 128, 17408], F32, kind="ExternalInput").ap()
    WFFN = nc.dram_tensor("wffn", [NEI * NFB, 128, 16384], F32, kind="ExternalInput").ap()
    BUD = nc.dram_tensor("bud", [128, NDC], F32, kind="ExternalInput").ap()
    BBC = nc.dram_tensor("bBC", [128, 256], F32, kind="ExternalInput").ap()
    ACOL = nc.dram_tensor("acol", [128, 128], F32, kind="ExternalInput").ap()
    DCOL = nc.dram_tensor("dcol", [128, NDC], F32, kind="ExternalInput").ap()
    B1C = nc.dram_tensor("b1c", [128, NEI * 32], F32, kind="ExternalInput").ap()
    B2BS2 = nc.dram_tensor("b2bs2", [9, 1024], F32, kind="ExternalInput").ap()
    GWH = nc.dram_tensor("gwh", [128, 64], F32, kind="ExternalInput").ap()
    ESEL = nc.dram_tensor("esel", [128, 1024], F32, kind="ExternalInput").ap()
    SSEL = nc.dram_tensor("ssel", [128, 1024], F32, kind="ExternalInput").ap()
    E9SEL = nc.dram_tensor("e9sel", [9, 1024], F32, kind="ExternalInput").ap()
    IDENTF = nc.dram_tensor("identf", [128, 128], F32, kind="ExternalInput").ap()
    CST = nc.dram_tensor("cst", [128, 8], F32, kind="ExternalInput").ap()
    ONES4096 = nc.dram_tensor("ones4096", [1, 4096], F32, kind="ExternalInput").ap()
    OUT = nc.dram_tensor("out", [BLOC, TSEQ, D], F32, kind="ExternalOutput").ap()

    ydram = nc.dram_tensor("ydram", [NTILE, 128, 1024], BF16).ap()
    xn2sp = nc.dram_tensor("xn2sp", [8, 128, HS], BF16).ap()
    cc_in_t = nc.dram_tensor("cc_in", [128, 256], F32)
    cc_out_t = nc.dram_tensor("cc_out", [1024, 256], F32)

    with ExitStack() as stk:
        E_ = stk.enter_context
        A2 = E_(nc.sbuf_tensor("A2", [128, 16384], BF16))
        A3 = E_(nc.sbuf_tensor("A3", [128, 16384], BF16))
        A4 = E_(nc.sbuf_tensor("A4", [128, 16384], BF16))
        AW = E_(nc.sbuf_tensor("AW", [128, 10240], BF16))
        AF32 = E_(nc.sbuf_tensor("AF32", [128, 5120], F32))
        AG = E_(nc.sbuf_tensor("AG", [128, 8192], BF16))
        AM = E_(nc.sbuf_tensor("AM", [128, 6144], BF16))
        ysb = E_(nc.sbuf_tensor("ysb", [128, 2048], F32))
        ytstg = E_(nc.sbuf_tensor("ytstg", [128, 1024], BF16))
        lgs = E_(nc.sbuf_tensor("lgs", [128, 256], F32))
        rows = E_(nc.sbuf_tensor("rows", [128, 16], F32))
        m2s = E_(nc.sbuf_tensor("m2s", [128, 32], F32))
        s2s = E_(nc.sbuf_tensor("s2s", [128, 32], F32))
        hstA = E_(nc.sbuf_tensor("hstA", [128, 128], F32))
        hstB = E_(nc.sbuf_tensor("hstB", [128, 128], F32))
        ones128 = E_(nc.sbuf_tensor("ones128", [128, 128], F32))
        bis = E_(nc.sbuf_tensor("bis", [128, 32], F32))
        buds = E_(nc.sbuf_tensor("buds", [128, NDC], F32))
        bBCs = E_(nc.sbuf_tensor("bBCs", [128, 256], F32))
        acols = E_(nc.sbuf_tensor("acols", [128, 128], F32))
        dcols = E_(nc.sbuf_tensor("dcols", [128, NDC], F32))
        b1cs = E_(nc.sbuf_tensor("b1cs", [128, NEI * 32], F32))
        b2s = E_(nc.sbuf_tensor("b2s", [9, 1024], BF16))
        gws = E_(nc.sbuf_tensor("gws", [128, 64], BF16))
        esels = E_(nc.sbuf_tensor("esels", [128, 1024], BF16))
        ssels = E_(nc.sbuf_tensor("ssels", [128, 1024], BF16))
        e9s = E_(nc.sbuf_tensor("e9s", [9, 1024], BF16))
        idf = E_(nc.sbuf_tensor("idf", [128, 128], F32))
        csts = E_(nc.sbuf_tensor("csts", [128, 8], F32))
        abar = E_(nc.sbuf_tensor("abar", [128, 1024], F32))
        ubcs = E_(nc.sbuf_tensor("ubcs", [128, 1024], F32))
        busb = E_(nc.sbuf_tensor("busb", [128, 512], F32))
        hsb = E_(nc.sbuf_tensor("hsb", [128, 1024], F32))
        spsb = E_(nc.sbuf_tensor("spsb", [128, 1024], F32))
        pB = [E_(nc.psum_tensor(f"pB{i}", [128, 512], F32)) for i in range(2)]
        pC = [E_(nc.psum_tensor(f"pC{i}", [128, 512], F32)) for i in range(2)]
        pD = E_(nc.psum_tensor("pD", [128, 512], F32))
        pS = E_(nc.psum_tensor("pS", [128, 512], F32))
        pY = [E_(nc.psum_tensor(f"pY{i}", [128, 512], F32)) for i in range(2)]

        ctx = Ctx(nc, stk)
        block = E_(nc.Block())

        def XN1(k):
            return A2[:, k * 2048:(k + 1) * 2048]

        def XN2(k):
            return A3[:, k * 2048:(k + 1) * 2048]

        def HBUF(j):
            return A2[:, j * 2048:(j + 1) * 2048]

        def OUTH(m):
            return A4[:, m * 2048:(m + 1) * 2048]

        u2 = AM[:, 0:2048]
        d2 = AM[:, 2048:4096]
        gate9 = AM[0:9, 0:4096]
        gbc = AM[:, 4096:6144]
        xstg = [AG[:, 0:1024], AG[:, 1024:2048]]
        ystg = [AG[:, 2048:3072], AG[:, 3072:4096]]
        cntjunk = AG[:, 2048:4096]
        chsb = [AG[:, 4096 + i * 512: 4096 + (i + 1) * 512] for i in range(2)]
        xbV = [AG[:, 0:2048], AG[:, 2048:4096]]
        sqV = [AG[:, 4096:6144], AG[:, 6144:8192]]
        wrL = [AW[:, s * 2048:(s + 1) * 2048] for s in range(4)]
        wrUD = [AW[:, 8192 + s * 1024: 8192 + (s + 1) * 1024] for s in range(2)]
        w1r = [AW[:, s * 1024:(s + 1) * 1024] for s in range(4)]
        w2r = [AW[:, 4096 + s * 1024: 4096 + (s + 1) * 1024] for s in range(4)]
        xn1f = [AF32[:, 0:1024], AF32[:, 1024:2048]]
        x1f = xn1f
        gath = AF32[:, 0:2048]
        trsrc = AF32[:, 0:1024]
        x1fin = AF32[:, 1024:2048]
        outtmV = [AF32[:, 2048:3072], AF32[:, 4096:5120]]
        STATJ = AF32[:, 3072:4096]
        mabar = [abar[:, 0:512], abar[:, 512:1024]]
        mubc = [ubcs[:, 0:512], ubcs[:, 512:1024]]
        mh = [hsb[:, 0:512], hsb[:, 512:1024]]
        spr = spsb[:, 0:512]
        spa = spsb[:, 512:1024]

        R_M1, R_R1, R_VAR, R_TMP, R_MY, R_RY, R_M3, R_R3 = range(8)

        def rc(i):
            return rows[:, i:i + 1]

        def csc(i):
            return csts[:, i:i + 1]

        C_EPS, C_NHALF, C_PHALF, C_GA, C_GS, C_NEG1, C_ONE = range(7)

        b_lo = bis[0:1, 0:1]
        b_hi = bis[0:1, 1:2]
        b_tau = bis[0:1, 2:3]
        b_m = bis[0:1, 3:4]
        b_t1 = bis[0:1, 4:5]
        b_t2 = bis[0:1, 5:6]
        b_tot = bis[0:1, 6:7]
        b_tcol = bis[:, 8:9]
        b_cnt = bis[:, 10:11]
        sm_e = bis[:, 16:24]
        sm_g = bis[:, 24:32]

        def x_tile_ap(h, tt):
            b = tt // 8
            t0 = h * HT + (tt % 8) * 128
            return X[b, t0:t0 + 128, :]

        def out_tile_ap(h, tt):
            b = tt // 8
            t0 = h * HT + (tt % 8) * 128
            return OUT[b, t0:t0 + 128, :]

        def gidx(dc, L, nt):
            return (dc * NLC + L) * NNT + nt

        def vg(v, inst, sem="vM"):
            ctx.inc(inst, sem, 1)
            if v is not None:
                v.wait_ge(ctx.h[sem], ctx.c[sem])

        def ln_stats(v, src_ap, mcol):
            n = src_ap.shape[-1] if v is not None else 0
            if v is None:
                ctx.c["vM"] += 7
                return
            vg(v, v.reduce_sum(rc(R_TMP), src_ap, AX.X))
            vg(v, v.tensor_tensor(STATJ[:, 0:n], src_ap, src_ap, ALU.mult))
            vg(v, v.reduce_sum(rc(R_VAR), STATJ[:, 0:n], AX.X))
            vg(v, v.tensor_single_scalar(mcol, rc(R_TMP), 1.0 / D, ALU.mult))
            vg(v, v.tensor_single_scalar(rc(R_VAR), rc(R_VAR), 1.0 / D, ALU.mult))
            vg(v, v.tensor_tensor(rc(R_TMP), mcol, mcol, ALU.mult))
            vg(v, v.tensor_tensor(rc(R_VAR), rc(R_VAR), rc(R_TMP), ALU.subtract))

        def ph_init(ctx, en, E):
            g = E if en == "gpsimd" else None
            q = E if en == "sync" else None
            v = E if en == "vector" else None
            pairs_q = [(buds[:], BUD), (bBCs[:], BBC), (acols[:], ACOL),
                       (dcols[:], DCOL), (b1cs[:], B1C), (idf[:], IDENTF),
                       (csts[:], CST)]
            for dst, src in pairs_q:
                ctx.inc(q and q.dma_start(out=dst, in_=src), "q", 16)
            pairs_g = [(b2s[:], B2BS2), (gws[:], GWH), (esels[:], ESEL),
                       (ssels[:], SSEL), (e9s[:], E9SEL)]
            for dst, src in pairs_g:
                ctx.inc(g and g.dma_start(out=dst, in_=src), "g", 16)
            ctx.mark("q", ("cst",))
            ctx.mark("g", ("cst",))
            if v:
                v.memset(ones128[:], 1.0)
                v.memset(hstA[:], 0.0)
                ctx.inc(v.memset(hstB[:], 0.0), "vM", 1)
            else:
                ctx.c["vM"] += 1

        def ph_ln1(ctx, en, E, h):
            import os as _os
            DBG = _os.environ.get("MK_DBG", "")
            g = E if en == "gpsimd" else None
            v = E if en == "vector" else None
            a = E if en == "scalar" else None
            t = E if en == "tensor" else None
            for tt in range(NTILE):
                if g:
                    ctx.wait(g, "vM", ("xdone", h, tt - 2))
                ctx.inc(g and g.dma_start(out=xstg[tt % 2][:],
                                          in_=x_tile_ap(h, tt)), "gX%d" % (tt % 2), 16)
                ctx.mark("gX%d" % (tt % 2), ("x", h, tt))
                if v:
                    ctx.wait(v, "gX%d" % (tt % 2), ("x", h, tt))
                    ctx.wait(v, "aM", ("r1", h, tt - 1))
                ln_stats(v, xstg[tt % 2][:] if v else None, rc(R_M1))
                ctx.mark("vM", ("st1", h, tt))
                if a:
                    if tt == 0:
                        ctx.wait(a, "q", ("cst",))
                    ctx.wait(a, "vM", ("st1", h, tt))
                    a.activation(rc(R_TMP), rc(R_VAR), AF.Ln, bias=csc(C_EPS))
                    ctx.inc(a.activation(rc(R_R1), rc(R_TMP), AF.Exp,
                                         scale=csc(C_NHALF)), "aM", 1)
                else:
                    ctx.c["aM"] += 1
                ctx.mark("aM", ("r1", h, tt))
                if v:
                    ctx.wait(v, "aM", ("r1", h, tt))
                    ctx.wait(v, "pT", ("tr", h, tt - 2))
                    ctx.inc(v.tensor_scalar(xn1f[tt % 2][:], xstg[tt % 2][:],
                                            rc(R_M1), rc(R_R1),
                                            ALU.subtract, ALU.mult), "vM", 1)
                else:
                    ctx.c["vM"] += 1
                ctx.mark("vM", ("xn1f", h, tt))
                ctx.mark("vM", ("xdone", h, tt))
                if DBG == "xstg":
                    if en == "gpsimd":
                        ctx.wait(E, "g", ("x", h, tt))
                        ctx.inc(E.dma_start(out=out_tile_ap(h, tt),
                                            in_=xstg[tt % 2][:]), "g", 16)
                    else:
                        ctx.c["g"] += 16
                    ctx.mark("g", ("dbgxs", h, tt))
                    if v:
                        ctx.wait(v, "g", ("dbgxs", h, tt))
                if DBG == "xn1":
                    if q_ := None:
                        pass
                    if en == "sync":
                        ctx.wait(E, "vM", ("xn1f", h, tt))
                    ctx.inc(E.dma_start(out=out_tile_ap(h, tt),
                                        in_=xn1f[tt % 2][:]) if en == "sync" else None,
                            "q", 16)
                    ctx.mark("q", ("dbgxn1", h, tt))
                    if v:
                        ctx.wait(v, "q", ("dbgxn1", h, tt))
                if t:
                    ctx.wait(t, "vM", ("xn1f", h, tt))
                    ctx.wait(t, "vT", ("trev", h, tt - 1))
                    last = None
                    for jj in range(8):
                        bank = pY[jj // 4]
                        last = t.transpose(bank[:, (jj % 4) * 128:(jj % 4 + 1) * 128],
                                           xn1f[tt % 2][:, jj * 128:(jj + 1) * 128],
                                           idf[:])
                    ctx.inc(last, "pT", 1)
                else:
                    ctx.c["pT"] += 1
                ctx.mark("pT", ("tr", h, tt))
                if v:
                    ctx.wait(v, "pT", ("tr", h, tt))
                    last = None
                    for jj in range(8):
                        bank = pY[jj // 4]
                        last = v.tensor_copy(XN1(jj)[:, tt * 128:(tt + 1) * 128],
                                             bank[:, (jj % 4) * 128:(jj % 4 + 1) * 128])
                    ctx.inc(last, "vT", 1)
                else:
                    ctx.c["vT"] += 1
                ctx.mark("vT", ("trev", h, tt))

        def ph_mamba(ctx, en, E, h):
            import os as _os
            DBG = _os.environ.get("MK_DBG", "")
            NDCR = min(NDC, int(_os.environ.get("MK_DCS", str(NDC))))
            SKIP = _os.environ.get("MK_SKIP", "")
            GLVL = int(_os.environ.get("MK_GRP", "6"))
            ELVL = int(_os.environ.get("MK_E", "4"))
            g = E if en == "gpsimd" else None
            q = E if en == "sync" else None
            v = E if en == "vector" else None
            a = E if en == "scalar" else None
            t = E if en == "tensor" else None

            base = dict(ctx.c)
            for dc in range(NDCR):
                if g and dc >= 2:
                    g.wait_ge(ctx.h["pUD"], base["pUD"] + (dc - 1) * 4)
                ctx.inc(g and g.dma_start(out=wrUD[dc % 2][:],
                                          in_=WINP[dc, :, 0:1024]),
                        "gU%d" % (dc % 2), 16)
                ctx.mark("gU%d" % (dc % 2), ("wud", h, dc))
                for L in range(NLC):
                    lg = dc * NLC + L
                    if g and lg >= 4:
                        g.wait_ge(ctx.h["pC"], base["pC"] + (lg - 3) * 4)
                    ctx.inc(g and g.dma_start(
                        out=wrL[lg % 4][:],
                        in_=WINP[dc, :, 1024 + L * 2048: 1024 + (L + 1) * 2048]),
                        "gL%d" % (lg % 4), 16)
                    ctx.mark("gL%d" % (lg % 4), ("wL", h, lg))

            for dc in range(NDCR):
                if q:
                    q.wait_ge(ctx.h["aM"], base["aM"] + dc * 8 + 7)
                ctx.inc(q and q.dma_start(out=u2[64:128, :], in_=u2[0:64, :]),
                        "qD", 16)
                ctx.mark("qD", ("dup", h, dc))
                if DBG == "ud" and dc == 0 and h == 0:
                    if g:
                        ctx.wait(g, "qD", ("dup", h, dc))
                        g.wait_ge(ctx.h["vM"], base["vM"] + 4)
                    ctx.inc(g and g.dma_start(out=OUT[0, 0:128, :],
                                              in_=u2[:, 0:1024]), "g", 16)
                    ctx.inc(g and g.dma_start(out=OUT[0, 128:256, :],
                                              in_=u2[:, 1024:2048]), "g", 16)
                    ctx.inc(g and g.dma_start(out=OUT[0, 256:384, :],
                                              in_=d2[:, 0:1024]), "g", 16)
                    ctx.inc(g and g.dma_start(out=OUT[0, 384:512, :],
                                              in_=d2[:, 1024:2048]), "g", 16)
                    ctx.mark("g", ("dbgud",))
                    if v:
                        ctx.wait(v, "g", ("dbgud",))
                if "e" not in SKIP and ELVL >= 4:
                    if q:
                        q.wait_ge(ctx.h["vT"], base["vT"] + dc + 1)
                    ctx.inc(q and q.dma_start(
                        out=ydram[:, :, dc * 64:(dc + 1) * 64].rearrange("s p f -> p s f"),
                        in_=ytstg[:].rearrange("p (s f) -> p s f", f=64)), "qY", 16)
                    ctx.mark("qY", ("ysp", h, dc))

            pend = []

            def emit_nsum(gp):
                dcp, rem = divmod(gp, NLC * NNT)
                Lp, ntp = divmod(rem, NNT)
                if t:
                    ctx.wait(t, "vC", ("ch", gp))
                    if Lp == 0:
                        ctx.wait(t, "vY", ("ysb", dcp - 1, 3))
                        ctx.wait(t, "vT", ("yev", h, dcp - 1))
                bank = pY[ntp // 2]
                hi = ntp % 2
                mm = t and t.matmul(
                    bank[64 * hi:64 * hi + 64, :],
                    ssels[:, Lp * 128 + 64 * hi: Lp * 128 + 64 * hi + 64],
                    chsb[gp % 2], start=(Lp == 0), stop=(Lp == NLC - 1),
                    skip_group_check=True)
                ctx.inc(mm, "pY", 1)
                ctx.mark("pY", ("nsum", gp))

            for dc in range(NDCR):
                for nt in range(0 if "u" in SKIP else NNT):
                    bank = pB[nt % 2]
                    nsl = slice(nt * 512, (nt + 1) * 512)
                    if t:
                        ctx.wait(t, "gU%d" % (dc % 2), ("wud", h, dc))
                        ctx.wait(t, "vB", ("budc", dc - 1))
                        ctx.wait(t, "aM", ("spl_a", h, dc - 1, 3))
                        ctx.wait(t, "vM", ("spl", h, dc - 1, 3))
                        if nt >= 2:
                            ctx.wait(t, "aM", ("spl_a", h, dc, nt - 2))
                            ctx.wait(t, "vM", ("spl", h, dc, nt - 2))
                        last = None
                        for k in range(8):
                            last = t.matmul(bank[:],
                                            wrUD[dc % 2][:, k * 128:(k + 1) * 128],
                                            XN1(k)[:, nsl],
                                            start=(k == 0), stop=(k == 7),
                                            skip_group_check=True)
                        ctx.inc(last, "pUD", 1)
                    else:
                        ctx.c["pUD"] += 1
                    ctx.mark("pUD", ("udmm1", h, dc, nt))
                    if nt == 3:
                        ctx.mark("pUD", ("udmm", h, dc))
                    UDLVL = int(_os.environ.get("MK_UD", "3"))
                    if a and UDLVL >= 2:
                        ctx.wait(a, "pUD", ("udmm1", h, dc, nt))
                        ctx.wait(a, "qD", ("dup", h, dc - 1))
                        ctx.wait(a, "vY", ("ysb", dc - 1, 3))
                        ctx.wait(a, "pU", ("pu", gidx(dc - 1, NLC - 1, 3)))
                        ctx.inc(a.activation(u2[0:64, nsl], bank[0:64, :], AF.Identity,
                                             bias=buds[0:64, dc:dc + 1]), "aM", 1)
                        ctx.wait(a, "vM", ("spl", h, dc, nt - 1))
                        a.activation(spa[64:128, :], bank[64:128, :], AF.Abs,
                                     bias=buds[64:128, dc:dc + 1])
                        a.activation(spa[64:128, :], spa[64:128, :], AF.Exp,
                                     scale=csc(C_NEG1)[64:128, :])
                        ctx.inc(a.activation(spa[64:128, :], spa[64:128, :], AF.Ln,
                                             bias=csc(C_ONE)[64:128, :]), "aM", 1)
                    elif UDLVL >= 2:
                        ctx.c["aM"] += 2
                    ctx.mark("aM", ("uev", h, dc, nt))
                    ctx.mark("aM", ("spl_a", h, dc, nt))
                    if v and UDLVL >= 3:
                        ctx.wait(v, "pUD", ("udmm1", h, dc, nt))
                        ctx.wait(v, "pD", ("pd", gidx(dc - 1, NLC - 1, nt)))
                        ctx.wait(v, "aM", ("spl_a", h, dc, nt))
                        v.tensor_scalar(spr[64:128, :], bank[64:128, :],
                                        buds[64:128, dc:dc + 1], 0.0,
                                        ALU.add, ALU.max)
                        ctx.inc(v.tensor_tensor(d2[64:128, nsl], spr[64:128, :],
                                                spa[64:128, :], ALU.add), "vM", 1)
                    elif UDLVL >= 3:
                        ctx.c["vM"] += 1
                    ctx.mark("vM", ("spl", h, dc, nt))

                for L in range(0 if "g" in SKIP else NLC):
                    lg = dc * NLC + L
                    for nt in range(NNT):
                        gp = gidx(dc, L, nt)
                        nsl = slice(nt * 512, (nt + 1) * 512)
                        dcl = dc * 8 + L
                        if pend and GLVL >= 6:
                            emit_nsum(pend.pop(0))
                        elif pend and GLVL < 6:
                            pend.pop(0)
                        if t:
                            ctx.wait(t, "gL%d" % (lg % 4), ("wL", h, lg))
                            ctx.wait(t, "vB", ("bu", gp - 2))
                            if L == 0:
                                ctx.wait(t, "aM", ("spl_a", h, dc, 2 + (nt % 2)))
                                ctx.wait(t, "vM", ("spl", h, dc, 2 + (nt % 2)))
                            last = None
                            for k in range(8):
                                last = t.matmul(pB[gp % 2][:],
                                                wrL[lg % 4][:, k * 128:(k + 1) * 128],
                                                XN1(k)[:, nsl],
                                                start=(k == 0), stop=(k == 7),
                                                skip_group_check=True)
                            ctx.inc(last, "pB", 1)
                        else:
                            ctx.c["pB"] += 1
                        ctx.mark("pB", ("pb", gp))
                        if t:
                            ctx.wait(t, "vC", ("ch", gp - 2))
                            last = None
                            for k in range(8):
                                last = t.matmul(
                                    pC[gp % 2][:],
                                    wrL[lg % 4][:, 1024 + k * 128: 1024 + (k + 1) * 128],
                                    XN1(k)[:, nsl],
                                    start=(k == 0), stop=(k == 7),
                                    skip_group_check=True)
                            ctx.inc(last, "pC", 1)
                        else:
                            ctx.c["pC"] += 1
                        ctx.mark("pC", ("pc", gp))
                        if nt == 3:
                            ctx.mark("pC", ("pcL", h, lg))
                        if t:
                            ctx.wait(t, "aA", ("ab", gp - 1))
                            ctx.wait(t, "vM", ("spl", h, dc, nt))
                            ctx.inc(t.matmul(pD[:],
                                             esels[64:128, L * 128:(L + 1) * 128],
                                             d2[64:128, nsl], start=True, stop=True,
                                             skip_group_check=True), "pD", 1)
                        else:
                            ctx.c["pD"] += 1
                        ctx.mark("pD", ("pd", gp))
                        if t:
                            ctx.wait(t, "aU", ("ue", gp - 1))
                            ctx.wait(t, "aM", ("uev", h, dc, nt))
                            ctx.inc(t.matmul(pS[:],
                                             esels[0:64, L * 128:(L + 1) * 128],
                                             u2[0:64, nsl], start=True, stop=True,
                                             skip_group_check=True), "pU", 1)
                        else:
                            ctx.c["pU"] += 1
                        ctx.mark("pU", ("pu", gp))
                        pend.append(gp)
                        if a and GLVL >= 2:
                            ctx.wait(a, "pD", ("pd", gp))
                            ctx.wait(a, "vS", ("sc", gp - 2))
                            ctx.inc(a.activation(mabar[gp % 2][:], pD[:], AF.Exp,
                                                 scale=acols[:, dcl:dcl + 1]),
                                    "aA", 1)
                            ctx.wait(a, "pU", ("pu", gp))
                            ctx.wait(a, "vB", ("bu", gp - 2))
                            ctx.inc(a.activation(mubc[gp % 2][:], pS[:], AF.Copy),
                                    "aU", 1)
                        elif GLVL >= 2:
                            ctx.c["aA"] += 1
                            ctx.c["aU"] += 1
                        ctx.mark("aA", ("ab", gp))
                        ctx.mark("aU", ("ue", gp))
                        if v and GLVL >= 3:
                            ctx.wait(v, "pB", ("pb", gp))
                            ctx.wait(v, "aU", ("ue", gp))
                            ctx.inc(v.scalar_tensor_tensor(
                                busb[:], pB[gp % 2][:], bBCs[:, dcl:dcl + 1],
                                mubc[gp % 2][:], ALU.add, ALU.mult), "vB", 1)
                        elif GLVL >= 3:
                            ctx.c["vB"] += 1
                        ctx.mark("vB", ("bu", gp))
                        if nt == 3 and L == NLC - 1:
                            ctx.mark("vB", ("budc", dc))
                        if v and GLVL >= 4:
                            ctx.wait(v, "aA", ("ab", gp))
                            ctx.wait(v, "vS", val=ctx.c["vS"])
                            if nt in (0, 2):
                                hst = hstA if nt == 0 else hstB
                                init = 0.0 if h == 0 else hst[:, dcl:dcl + 1]
                            else:
                                init = mh[(gp - 1) % 2][:, 511:512]
                            ctx.inc(v.tensor_tensor_scan(
                                mh[gp % 2][:], mabar[gp % 2][:], busb[:],
                                init, ALU.mult, ALU.add), "vS", 1)
                        elif GLVL >= 4:
                            ctx.c["vS"] += 1
                        ctx.mark("vS", ("sc", gp))
                        if nt in (1, 3):
                            if v and GLVL >= 4:
                                ctx.wait(v, "vS", val=ctx.c["vS"])
                                hst = hstA if nt == 1 else hstB
                                ctx.inc(v.tensor_copy(hst[:, dcl:dcl + 1],
                                                      mh[gp % 2][:, 511:512]),
                                        "vM", 1)
                            elif GLVL >= 4:
                                ctx.c["vM"] += 1
                        if v and GLVL >= 5:
                            ctx.wait(v, "pC", ("pc", gp))
                            ctx.wait(v, "vS", val=ctx.c["vS"])
                            ctx.wait(v, "pY", ("nsum", gp - 2))
                            ctx.inc(v.scalar_tensor_tensor(
                                chsb[gp % 2], pC[gp % 2][:],
                                bBCs[:, 128 + dcl:129 + dcl], mh[gp % 2][:],
                                ALU.add, ALU.mult), "vC", 1)
                        elif GLVL >= 5:
                            ctx.c["vC"] += 1
                        ctx.mark("vC", ("ch", gp))

                for nt in range(0 if "e" in SKIP else NNT):
                    gp_last = gidx(dc, NLC - 1, nt)
                    hi = nt % 2
                    rsl = slice(64 * hi, 64 * hi + 64)
                    nsl = slice(nt * 512, (nt + 1) * 512)
                    if pend and pend[0] <= gp_last:
                        while pend and pend[0] <= gp_last:
                            emit_nsum(pend.pop(0))
                    if v:
                        ctx.wait(v, "pY", ("nsum", gp_last))
                        if hi == 1:
                            ctx.wait(v, "qD", ("dup", h, dc))
                        ctx.wait(v, "pT", ("ytr", dc - 1))
                        ctx.inc(v.scalar_tensor_tensor(
                            ysb[rsl, nsl], u2[rsl, nsl], dcols[rsl, dc:dc + 1],
                            pY[nt // 2][rsl, :], ALU.mult, ALU.add), "vY", 1)
                    else:
                        ctx.c["vY"] += 1
                    ctx.mark("vY", ("ysb", dc, nt))
                if t and "e" not in SKIP and ELVL >= 2:
                    last = None
                    for st in range(16):
                        ntq = st // 4
                        qq = st % 4
                        hi = ntq % 2
                        if st == 0:
                            ctx.wait(t, "vY", ("ysb", dc, 3))
                            ctx.wait(t, "vT", ("yev", h, dc - 1))
                        bank = pY[st // 8]
                        last = t.transpose(
                            bank[0:128, (st % 8) * 64:(st % 8) * 64 + 64],
                            ysb[:, ntq * 512 + qq * 128: ntq * 512 + (qq + 1) * 128],
                            idf[:, 64 * hi:64 * hi + 64])
                    ctx.inc(last, "pT", 1)
                elif "e" not in SKIP and ELVL >= 2:
                    ctx.c["pT"] += 1
                ctx.mark("pT", ("ytr", dc))
                if v and "e" not in SKIP and ELVL >= 3:
                    ctx.wait(v, "pT", ("ytr", dc))
                    ctx.wait(v, "qY", ("ysp", h, dc - 1))
                    last = None
                    for st in range(16):
                        bank = pY[st // 8]
                        last = v.tensor_copy(ytstg[:, st * 64:(st + 1) * 64],
                                             bank[:, (st % 8) * 64:(st % 8) * 64 + 64])
                    ctx.inc(last, "vT", 1)
                elif "e" not in SKIP and ELVL >= 3:
                    ctx.c["vT"] += 1
                ctx.mark("vT", ("yev", h, dc))
            while pend:
                emit_nsum(pend.pop(0))

        def ph_post(ctx, en, E, h):
            import os as _os
            DBG = _os.environ.get("MK_DBG", "")
            g = E if en == "gpsimd" else None
            q = E if en == "sync" else None
            v = E if en == "vector" else None
            a = E if en == "scalar" else None
            t = E if en == "tensor" else None
            XO = XN2 if h == 0 else XN1

            for tt in range(NTILE):
                if g:
                    ctx.wait(g, "vM", ("px", h, tt - 2))
                ctx.inc(g and g.dma_start(out=xstg[tt % 2][:],
                                          in_=x_tile_ap(h, tt)),
                        "gX%d" % (tt % 2), 16)
                ctx.mark("gX%d" % (tt % 2), ("pxd", h, tt))
                if q:
                    ctx.wait(q, "vM", ("py", h, tt - 2))
                ctx.inc(q and q.dma_start(out=ystg[tt % 2][:], in_=ydram[tt]),
                        "qP%d" % (tt % 2), 16)
                ctx.mark("qP%d" % (tt % 2), ("pyd", h, tt))
                if DBG == "y":
                    if g:
                        ctx.wait(g, "qP%d" % (tt % 2), ("pyd", h, tt))
                    ctx.inc(g and g.dma_start(out=out_tile_ap(h, tt),
                                              in_=ystg[tt % 2][:]), "g", 16)
                if v:
                    ctx.wait(v, "qP%d" % (tt % 2), ("pyd", h, tt))
                    ctx.wait(v, "aM", ("s2", h, tt - 1))
                ln_stats(v, ystg[tt % 2][:] if v else None, rc(R_MY))
                ctx.mark("vM", ("sty", h, tt))
                if a:
                    ctx.wait(a, "vM", ("sty", h, tt))
                    a.activation(rc(R_TMP), rc(R_VAR), AF.Ln, bias=csc(C_EPS))
                    ctx.inc(a.activation(rc(R_RY), rc(R_TMP), AF.Exp,
                                         scale=csc(C_NHALF)), "aM", 1)
                else:
                    ctx.c["aM"] += 1
                ctx.mark("aM", ("ry", h, tt))
                if v:
                    ctx.wait(v, "gX%d" % (tt % 2), ("pxd", h, tt))
                    ctx.wait(v, "aM", ("ry", h, tt))
                    ctx.wait(v, "pT", ("ptr", h, tt - 2))
                    v.tensor_scalar(x1f[tt % 2][:], ystg[tt % 2][:], rc(R_MY),
                                    rc(R_RY), ALU.subtract, ALU.mult)
                    v.tensor_tensor(x1f[tt % 2][:], x1f[tt % 2][:],
                                    xstg[tt % 2][:], ALU.add)
                ln_stats(v, x1f[tt % 2][:] if v else None,
                         m2s[:, h * 16 + tt: h * 16 + tt + 1])
                ctx.mark("vM", ("st2", h, tt))
                ctx.mark("vM", ("py", h, tt))
                ctx.mark("vM", ("px", h, tt))
                if DBG == "x1":
                    if q:
                        ctx.wait(q, "vM", ("st2", h, tt))
                    ctx.inc(q and q.dma_start(out=out_tile_ap(h, tt),
                                              in_=x1f[tt % 2][:]), "q", 16)
                    ctx.mark("q", ("dbgx1", h, tt))
                if v and DBG == "x1":
                    ctx.wait(v, "q", ("dbgx1", h, tt))
                if a:
                    ctx.wait(a, "vM", ("st2", h, tt))
                    a.activation(rc(R_TMP), rc(R_VAR), AF.Ln, bias=csc(C_EPS))
                    a.activation(s2s[:, h * 16 + tt: h * 16 + tt + 1], rc(R_TMP),
                                 AF.Exp, scale=csc(C_PHALF))
                    ctx.inc(a.activation(rc(R_R1), rc(R_TMP), AF.Exp,
                                         scale=csc(C_NHALF)), "aM", 1)
                else:
                    ctx.c["aM"] += 1
                ctx.mark("aM", ("s2", h, tt))
                if v:
                    ctx.wait(v, "aM", ("s2", h, tt))
                    ctx.inc(v.tensor_scalar(
                        x1f[tt % 2][:], x1f[tt % 2][:],
                        m2s[:, h * 16 + tt: h * 16 + tt + 1], rc(R_R1),
                        ALU.subtract, ALU.mult), "vM", 1)
                else:
                    ctx.c["vM"] += 1
                ctx.mark("vM", ("xn2", h, tt))
                if t:
                    ctx.wait(t, "vM", ("xn2", h, tt))
                    ctx.wait(t, "vT", ("pev", h, tt - 1))
                    last = None
                    for jj in range(8):
                        bank = pY[jj // 4]
                        last = t.transpose(bank[:, (jj % 4) * 128:(jj % 4 + 1) * 128],
                                           x1f[tt % 2][:, jj * 128:(jj + 1) * 128],
                                           idf[:])
                    ctx.inc(last, "pT", 1)
                else:
                    ctx.c["pT"] += 1
                ctx.mark("pT", ("ptr", h, tt))
                if v:
                    ctx.wait(v, "pT", ("ptr", h, tt))
                    last = None
                    for jj in range(8):
                        bank = pY[jj // 4]
                        last = v.tensor_copy(XO(jj)[:, tt * 128:(tt + 1) * 128],
                                             bank[:, (jj % 4) * 128:(jj % 4 + 1) * 128])
                    ctx.inc(last, "vT", 1)
                else:
                    ctx.c["vT"] += 1
                ctx.mark("vT", ("pev", h, tt))
                if t:
                    ctx.wait(t, "vT", ("pev", h, tt))
                    ctx.wait(t, "vM", ("lgev", h, tt - 1))
                    last = None
                    for k in range(8):
                        last = t.matmul(pS[:, 0:8],
                                        XO(k)[:, tt * 128:(tt + 1) * 128],
                                        gws[:, k * 8:(k + 1) * 8],
                                        start=(k == 0), stop=(k == 7),
                                        skip_group_check=True)
                    ctx.inc(last, "pG", 1)
                else:
                    ctx.c["pG"] += 1
                ctx.mark("pG", ("lg", h, tt))
                if v:
                    ctx.wait(v, "pG", ("lg", h, tt))
                    ctx.inc(v.tensor_copy(
                        lgs[:, (h * 16 + tt) * 8:(h * 16 + tt + 1) * 8],
                        pS[:, 0:8]), "vM", 1)
                else:
                    ctx.c["vM"] += 1
                ctx.mark("vM", ("lgev", h, tt))
            if h == 1:
                for k in range(8):
                    if q:
                        ctx.wait(q, "vT", ("pev", h, NTILE - 1))
                    ctx.inc(q and q.dma_start(out=xn2sp[k], in_=XO(k)), "q", 16)

        def ph_gate(ctx, en, E):
            import os as _os
            DBG = _os.environ.get("MK_DBG", "")
            g = E if en == "gpsimd" else None
            q = E if en == "sync" else None
            v = E if en == "vector" else None
            a = E if en == "scalar" else None
            t = E if en == "tensor" else None
            ctx.inc(g and g.dma_start(out=gate9[8:9, :], in_=ONES4096), "g", 16)
            ctx.mark("g", ("g9one",))
            for i in range(32):
                sl = slice(i * 8, (i + 1) * 8)
                if v:
                    ctx.wait(v, "aX", ("sm2", i - 1))
                    vg(v, v.reduce_max(rc(R_TMP), lgs[:, sl], AX.X), "vX")
                    vg(v, v.tensor_single_scalar(rc(R_TMP), rc(R_TMP), -1.0,
                                                 ALU.mult), "vX")
                else:
                    ctx.c["vX"] += 2
                ctx.mark("vX", ("sm1", i))
                if a:
                    ctx.wait(a, "vX", ("sm1", i))
                    ctx.inc(a.activation(sm_e, lgs[:, sl], AF.Exp,
                                         bias=rc(R_TMP)), "aX", 1)
                else:
                    ctx.c["aX"] += 1
                ctx.mark("aX", ("sm2", i))
                if v:
                    ctx.wait(v, "aX", ("sm2", i))
                    vg(v, v.reduce_sum(rc(R_VAR), sm_e, AX.X), "vX")
                    vg(v, v.reciprocal(rc(R_VAR), rc(R_VAR)), "vX")
                    vg(v, v.tensor_single_scalar(lgs[:, sl], sm_e,
                                                 rc(R_VAR), ALU.mult), "vX")
                else:
                    ctx.c["vX"] += 3
                ctx.mark("vX", ("sm3", i))
            if q:
                ctx.wait(q, "vX", ("sm3", 31))
            ctx.inc(q and q.dma_start(out=cc_in_t.ap(), in_=lgs[:]), "q", 16)
            ctx.mark("q", ("ccin",))
            if g:
                ctx.wait(g, "q", ("ccin",))
                ctx.inc(g.collective_compute(
                    "AllGather", ALU.bypass, replica_groups=[list(range(8))],
                    ins=[cc_in_t.ap().opt()], outs=[cc_out_t.ap().opt()]), "cc", 1)
                ctx.wait(g, "cc", val=ctx.c["cc"])
                ctx.inc(g.dma_start(
                    out=gath, in_=cc_out_t.ap().rearrange("(r p) f -> p r f", r=8)),
                    "g", 16)
            else:
                ctx.c["cc"] += 1
                ctx.c["g"] += 16
            ctx.mark("g", ("gath",))
            if v:
                ctx.wait(v, "g", ("gath",))
                v.memset(b_lo, 0.0)
                v.memset(b_hi, 1.0)
                v.memset(b_tau, 0.5)
                ctx.inc(v.memset(b_tcol, 0.5), "vX", 1)
            else:
                ctx.c["vX"] += 1
            for r in range(NBIS):
                if v:
                    ctx.wait(v, "vX", val=ctx.c["vX"])
                    vg(v, v.tensor_single_scalar(AF32[:, 2048:4096], gath, b_tcol,
                                                 ALU.is_gt), "vX")
                    vg(v, v.reduce_sum(b_cnt, AF32[:, 2048:4096], AX.X), "vX")
                else:
                    ctx.c["vX"] += 2
                ctx.mark("vX", ("cnt", r))
                if t:
                    ctx.wait(t, "vX", ("cnt", r))
                    ctx.inc(t.matmul(pS[0:1, 16:17], b_cnt, ones128[:, 0:1],
                                     start=True, stop=True,
                                     skip_group_check=True), "pX", 1)
                else:
                    ctx.c["pX"] += 1
                ctx.mark("pX", ("tot", r))
                if a:
                    ctx.wait(a, "pX", ("tot", r))
                    ctx.inc(a.copy(b_tot, pS[0:1, 16:17]), "aX", 1)
                else:
                    ctx.c["aX"] += 1
                ctx.mark("aX", ("tot", r))
                if v:
                    ctx.wait(v, "aX", ("tot", r))
                    vg(v, v.tensor_single_scalar(b_m, b_tot, KTOP, ALU.is_ge), "vX")
                    vg(v, v.tensor_tensor(b_t1, b_tau, b_lo, ALU.subtract), "vX")
                    vg(v, v.tensor_tensor(b_t2, b_m, b_t1, ALU.mult), "vX")
                    vg(v, v.tensor_tensor(b_lo, b_lo, b_t2, ALU.add), "vX")
                    vg(v, v.tensor_tensor(b_t1, b_hi, b_tau, ALU.subtract), "vX")
                    vg(v, v.tensor_scalar(b_m, b_m, -1.0, 1.0, ALU.mult, ALU.add), "vX")
                    vg(v, v.tensor_tensor(b_t2, b_m, b_t1, ALU.mult), "vX")
                    vg(v, v.tensor_tensor(b_hi, b_hi, b_t2, ALU.subtract), "vX")
                    vg(v, v.tensor_tensor(b_t1, b_lo, b_hi, ALU.add), "vX")
                    vg(v, v.tensor_single_scalar(b_tau, b_t1, 0.5, ALU.mult), "vX")
                else:
                    ctx.c["vX"] += 10
                ctx.mark("vX", ("upd", r))
                if t:
                    ctx.wait(t, "vX", ("upd", r))
                    src = b_lo if r == NBIS - 1 else b_tau
                    ctx.inc(t.matmul(pS[:, 17:18], ones128[0:1, 0:128], src,
                                     start=True, stop=True,
                                     skip_group_check=True), "pX", 1)
                else:
                    ctx.c["pX"] += 1
                ctx.mark("pX", ("bc", r))
                if v:
                    ctx.wait(v, "pX", ("bc", r))
                    ctx.inc(v.tensor_copy(b_tcol, pS[:, 17:18]), "vX", 1)
                else:
                    ctx.c["vX"] += 1
            for i in range(32):
                sl = slice(i * 8, (i + 1) * 8)
                if v:
                    ctx.wait(v, "pX", ("gtr", i - 1))
                    vg(v, v.tensor_single_scalar(sm_e, lgs[:, sl], b_tcol,
                                                 ALU.is_gt), "vX")
                    vg(v, v.tensor_tensor(sm_g, sm_e, lgs[:, sl], ALU.mult), "vX")
                else:
                    ctx.c["vX"] += 2
                ctx.mark("vX", ("gm", i))
                if t:
                    ctx.wait(t, "vX", ("gm", i))
                    ctx.inc(t.transpose(pS[0:8, 32:160], sm_g, idf[:]), "pX", 1)
                else:
                    ctx.c["pX"] += 1
                ctx.mark("pX", ("gtr", i))
                if v:
                    ctx.wait(v, "pX", ("gtr", i))
                    ctx.wait(v, "g", ("g9one",))
                    ctx.inc(v.tensor_copy(gate9[0:8, i * 128:(i + 1) * 128],
                                          pS[0:8, 32:160]), "vX", 1)
                else:
                    ctx.c["vX"] += 1
                ctx.mark("vX", ("g9", i))
            if DBG == "gate":
                for qq in range(4):
                    if g:
                        ctx.wait(g, "vX", ("g9", 31))
                    ctx.inc(g and g.dma_start(
                        out=OUT[0, qq * 9:(qq + 1) * 9, :],
                        in_=gate9[:, qq * 1024:(qq + 1) * 1024]), "g", 16)
                ctx.mark("g", ("dbgg",))
                if v:
                    ctx.wait(v, "g", ("dbgg",))

        def ph_ffn(ctx, en, E, h):
            import os as _os
            DBG = _os.environ.get("MK_DBG", "")
            g = E if en == "gpsimd" else None
            q = E if en == "sync" else None
            v = E if en == "vector" else None
            a = E if en == "scalar" else None
            t = E if en == "tensor" else None
            aG0 = ctx.c["aG"]
            vA0 = ctx.c["vA"]
            ctx_w1base = ctx.c["pW1"]
            ctx_w2base = ctx.c["pW2"]

            if h == 1:
                for k in range(8):
                    ctx.inc(q and q.dma_start(out=XN2(k), in_=xn2sp[k]), "q", 16)
                ctx.mark("q", ("rld",))

            for ei in range(NEI):
                for fb in range(NFB):
                    gi = ei * NFB + fb
                    for j in range(8):
                        jseq = gi * 8 + j
                        if g and jseq >= 4:
                            g.wait_ge(ctx.h["pW1"], aG0 * 0 + ctx_w1base + (jseq - 3) * 4)
                        ctx.inc(g and g.dma_start(
                            out=w1r[jseq % 4][:],
                            in_=WFFN[gi, :, j * 1024:(j + 1) * 1024]),
                            "gW1%d" % (jseq % 4), 16)
                        ctx.mark("gW1%d" % (jseq % 4), ("w1", gi, j))
                    for m in range(8):
                        mseq = gi * 8 + m
                        if g and mseq >= 4:
                            g.wait_ge(ctx.h["pW2"], ctx_w2base + (mseq - 3) * 4)
                        ctx.inc(g and g.dma_start(
                            out=w2r[mseq % 4][:],
                            in_=WFFN[gi, :, 8192 + m * 1024: 8192 + (m + 1) * 1024]),
                            "gW2%d" % (mseq % 4), 16)
                        ctx.mark("gW2%d" % (mseq % 4), ("w2", gi, m))

            w1seq = 0
            w2seq = 0
            for ei in range(NEI):
                if ei > 0:
                    for qq in range(4):
                        if t:
                            ctx.wait(t, "vM", ("gbcev", ei - 1, 3))
                            ctx.wait(t, "vM", ("gbcev", ei, qq - 1))
                            ctx.inc(t.matmul(
                                pD[:], e9s[0:9, (ei - 1) * 128: ei * 128],
                                gate9[0:9, h * 2048 + qq * 512:
                                      h * 2048 + (qq + 1) * 512],
                                start=True, stop=True,
                                skip_group_check=True), "pG", 1)
                        else:
                            ctx.c["pG"] += 1
                        ctx.mark("pG", ("gbc", ei, qq))
                        if v:
                            ctx.wait(v, "pG", ("gbc", ei, qq))
                            ctx.inc(v.tensor_copy(gbc[:, qq * 512:(qq + 1) * 512],
                                                  pD[:]), "vM", 1)
                        else:
                            ctx.c["vM"] += 1
                        ctx.mark("vM", ("gbcev", ei, qq))
                for fb in range(NFB):
                    gi = ei * NFB + fb
                    for j in range(8):
                        jseq = gi * 8 + j
                        bcol = b1cs[:, ei * 32 + fb * 8 + j: ei * 32 + fb * 8 + j + 1]
                        for nt in range(NNT):
                            gsl = slice(nt * 512, (nt + 1) * 512)
                            bank = pB[w1seq % 2]
                            if t:
                                ctx.wait(t, "gW1%d" % (jseq % 4), ("w1", gi, j))
                                if h == 1 and gi == 0 and j == 0 and nt == 0:
                                    ctx.wait(t, "q", ("rld",))
                                ctx.wait(t, "aG", val=max(aG0, aG0 + w1seq - 1))
                                last = None
                                for k in range(8):
                                    last = t.matmul(
                                        bank[:],
                                        w1r[jseq % 4][:, k * 128:(k + 1) * 128],
                                        XN2(k)[:, gsl],
                                        start=(k == 0), stop=(k == 7),
                                        skip_group_check=True)
                                ctx.inc(last, "pW1", 1)
                            else:
                                ctx.c["pW1"] += 1
                            ctx.mark("pW1", ("w1g", gi, j, nt))
                            if nt == 3:
                                ctx.mark("pW1", ("w1f", jseq))
                            if a:
                                ctx.wait(a, "pW1", ("w1g", gi, j, nt))
                                ctx.wait(a, "vH", ("hp", jseq - 2))
                                a.activation(xbV[j % 2][:, gsl], bank[:], AF.Identity,
                                             bias=bcol)
                                ctx.wait(a, "vZ", ("p4", jseq - 2))
                                ctx.inc(a.activation(sqV[j % 2][:, gsl], bank[:],
                                                     AF.Square, bias=bcol), "aG", 1)
                            else:
                                ctx.c["aG"] += 1
                            w1seq += 1
                        if v:
                            ctx.wait(v, "aG", val=aG0 + w1seq)
                            v.tensor_scalar(sqV[j % 2][:], sqV[j % 2][:], GC_A, 1.0,
                                            ALU.mult, ALU.add)
                            ctx.inc(v.tensor_tensor(sqV[j % 2][:], sqV[j % 2][:],
                                                    xbV[j % 2][:], ALU.mult), "vZ", 1)
                        else:
                            ctx.c["vZ"] += 1
                        ctx.mark("vZ", ("p4", jseq))
                        if a:
                            ctx.wait(a, "vZ", ("p4", jseq))
                            ctx.inc(a.activation(sqV[j % 2][:], sqV[j % 2][:],
                                                 AF.Sigmoid, scale=csc(C_GS)),
                                    "aS", 1)
                        else:
                            ctx.c["aS"] += 1
                        ctx.mark("aS", ("p5", jseq))
                        if v:
                            ctx.wait(v, "aS", ("p5", jseq))
                            ctx.wait(v, "pW2", ("w2last", gi - 1))
                            if ei == 0:
                                ctx.inc(v.tensor_tensor(HBUF(j), sqV[j % 2][:],
                                                        xbV[j % 2][:], ALU.mult),
                                        "vH", 1)
                            else:
                                v.tensor_tensor(sqV[j % 2][:], sqV[j % 2][:],
                                                xbV[j % 2][:], ALU.mult)
                                ctx.wait(v, "vM", ("gbcev", ei, 3))
                                ctx.inc(v.tensor_tensor(HBUF(j), sqV[j % 2][:],
                                                        gbc[:], ALU.mult), "vH", 1)
                        else:
                            ctx.c["vH"] += 1
                        ctx.mark("vH", ("hp", jseq))
                        ctx.mark("vH", ("hpk", gi, j))
                    for m in range(8):
                        mseq = gi * 8 + m
                        for nt in range(NNT):
                            gsl = slice(nt * 512, (nt + 1) * 512)
                            bank = pC[w2seq % 2]
                            if t:
                                ctx.wait(t, "gW2%d" % (mseq % 4), ("w2", gi, m))
                                ctx.wait(t, "vA", val=max(vA0, vA0 + w2seq - 1))
                                if m == 0 and nt == 0:
                                    for k in range(8):
                                        ctx.wait(t, "vH", ("hpk", gi, k))
                                else:
                                    ctx.wait(t, "vH", ("hpk", gi, 7))
                                nk9 = (ei == NEI - 1 and fb == NFB - 1)
                                last = None
                                for k in range(8):
                                    last = t.matmul(
                                        bank[:],
                                        w2r[mseq % 4][:, k * 128:(k + 1) * 128],
                                        HBUF(k)[:, gsl],
                                        start=(k == 0),
                                        stop=(k == 7 and not nk9),
                                        skip_group_check=True)
                                if nk9:
                                    last = t.matmul(
                                        bank[:], b2s[0:9, m * 128:(m + 1) * 128],
                                        gate9[0:9, h * 2048 + nt * 512:
                                              h * 2048 + (nt + 1) * 512],
                                        start=False, stop=True,
                                        skip_group_check=True)
                                ctx.inc(last, "pW2", 1)
                            else:
                                ctx.c["pW2"] += 1
                            ctx.mark("pW2", ("w2g", gi, m, nt))
                            if nt == 3:
                                ctx.mark("pW2", ("w2f", mseq))
                                if m == 7:
                                    ctx.mark("pW2", ("w2last", gi))
                            if v:
                                ctx.wait(v, "pW2", ("w2g", gi, m, nt))
                                if ei == 0 and fb == 0:
                                    ctx.inc(v.tensor_copy(OUTH(m)[:, gsl], bank[:]),
                                            "vA", 1)
                                else:
                                    ctx.inc(v.tensor_tensor(OUTH(m)[:, gsl], bank[:],
                                                            OUTH(m)[:, gsl],
                                                            ALU.add), "vA", 1)
                            else:
                                ctx.c["vA"] += 1
                            ctx.mark("vA", ("addlast",))
                            w2seq += 1
            if DBG == "moe":
                for m in range(8):
                    if g:
                        ctx.wait(g, "vA", ("addlast",))
                    ctx.inc(g and g.dma_start(
                        out=OUT[0, m * 256:(m + 1) * 256, :].rearrange(
                            "(p c) d -> p (c d)", c=2),
                        in_=OUTH(m)), "g", 16)
                ctx.mark("g", ("dbgmoe",))
                if v:
                    ctx.wait(v, "g", ("dbgmoe",))

        def ph_ffn_tail(ctx, en, E, h):
            pass

        def ph_fin(ctx, en, E, h):
            q = E if en == "sync" else None
            v = E if en == "vector" else None
            a = E if en == "scalar" else None
            t = E if en == "tensor" else None
            for tt in range(NTILE):
                tsl = slice(tt * 128, (tt + 1) * 128)
                if v:
                    ctx.wait(v, "pT", ("ftr2", h, tt - 1))
                    last = None
                    for m in range(8):
                        last = v.tensor_copy(trsrc[:, m * 128:(m + 1) * 128],
                                             XN2(m)[:, tsl])
                    ctx.inc(last, "vM", 1)
                else:
                    ctx.c["vM"] += 1
                ctx.mark("vM", ("fx", h, tt))
                if t:
                    ctx.wait(t, "vM", ("fx", h, tt))
                    ctx.wait(t, "vM", ("foev", h, tt - 1))
                    last = None
                    for m in range(8):
                        bank = pY[m // 4]
                        last = t.transpose(bank[:, (m % 4) * 128:(m % 4 + 1) * 128],
                                           trsrc[:, m * 128:(m + 1) * 128], idf[:])
                    ctx.inc(last, "pT", 1)
                else:
                    ctx.c["pT"] += 1
                ctx.mark("pT", ("ftr1", h, tt))
                if v:
                    ctx.wait(v, "pT", ("ftr1", h, tt))
                    last = None
                    for m in range(8):
                        bank = pY[m // 4]
                        last = v.tensor_scalar(
                            x1fin[:, m * 128:(m + 1) * 128],
                            bank[:, (m % 4) * 128:(m % 4 + 1) * 128],
                            s2s[:, h * 16 + tt: h * 16 + tt + 1],
                            m2s[:, h * 16 + tt: h * 16 + tt + 1],
                            ALU.mult, ALU.add)
                    ctx.inc(last, "vM", 1)
                else:
                    ctx.c["vM"] += 1
                ctx.mark("vM", ("fx1", h, tt))
                if v:
                    last = None
                    for m in range(8):
                        last = v.tensor_copy(trsrc[:, m * 128:(m + 1) * 128],
                                             OUTH(m)[:, tsl])
                    ctx.inc(last, "vM", 1)
                else:
                    ctx.c["vM"] += 1
                ctx.mark("vM", ("fo", h, tt))
                if t:
                    ctx.wait(t, "vM", ("fo", h, tt))
                    ctx.wait(t, "vM", ("fx1", h, tt))
                    last = None
                    for m in range(8):
                        bank = pY[m // 4]
                        last = t.transpose(bank[:, (m % 4) * 128:(m % 4 + 1) * 128],
                                           trsrc[:, m * 128:(m + 1) * 128], idf[:])
                    ctx.inc(last, "pT", 1)
                else:
                    ctx.c["pT"] += 1
                ctx.mark("pT", ("ftr2", h, tt))
                if v:
                    ctx.wait(v, "pT", ("ftr2", h, tt))
                    ctx.wait(v, "qO%d" % (tt % 2), ("outd", h, tt - 2))
                    ctx.wait(v, "aM", ("fr3", h, tt - 1))
                    last = None
                    for m in range(8):
                        bank = pY[m // 4]
                        last = v.tensor_copy(outtmV[tt % 2][:, m * 128:(m + 1) * 128],
                                             bank[:, (m % 4) * 128:(m % 4 + 1) * 128])
                ln_stats(v, outtmV[tt % 2][:] if v else None, rc(R_M3))
                ctx.mark("vM", ("foev", h, tt))
                ctx.mark("vM", ("fst", h, tt))
                if a:
                    ctx.wait(a, "vM", ("fst", h, tt))
                    a.activation(rc(R_TMP), rc(R_VAR), AF.Ln, bias=csc(C_EPS))
                    ctx.inc(a.activation(rc(R_R3), rc(R_TMP), AF.Exp,
                                         scale=csc(C_NHALF)), "aM", 1)
                else:
                    ctx.c["aM"] += 1
                ctx.mark("aM", ("fr3", h, tt))
                if v:
                    ctx.wait(v, "aM", ("fr3", h, tt))
                    v.tensor_scalar(outtmV[tt % 2][:], outtmV[tt % 2][:],
                                    rc(R_M3), rc(R_R3), ALU.subtract, ALU.mult)
                    ctx.inc(v.tensor_tensor(outtmV[tt % 2][:], outtmV[tt % 2][:],
                                            x1fin[:], ALU.add), "vM", 1)
                else:
                    ctx.c["vM"] += 1
                ctx.mark("vM", ("fdone", h, tt))
                if q:
                    ctx.wait(q, "vM", ("fdone", h, tt))
                ctx.inc(q and q.dma_start(out=out_tile_ap(h, tt),
                                          in_=outtmV[tt % 2][:]),
                        "qO%d" % (tt % 2), 16)
                ctx.mark("qO%d" % (tt % 2), ("outd", h, tt))

        def ph_end(ctx, en, E):
            if en == "sync":
                for s in SEMS:
                    if (s[0] in "gq") and ctx.c[s] > 0:
                        E.wait_ge(ctx.h[s], ctx.c[s])

        def run_phase(fn, *args):
            snap = ctx.snap()
            final = None
            for en in ("gpsimd", "sync", "tensor", "scalar", "vector"):
                ctx.load(snap)

                def body(EE, en=en):
                    for s in SEMS:
                        if snap[s] > 0:
                            EE.wait_ge(ctx.h[s], snap[s])
                    fn(ctx, en, EE, *args)

                getattr(block, en)(body)
                if final is None:
                    final = ctx.snap()
                else:
                    assert final == ctx.snap(), (
                        f"phase {fn.__name__} counter divergence on {en}: "
                        f"{ {k: (final[k], ctx.c[k]) for k in final if final[k] != ctx.c[k]} }")

        import os as _os
        _plim = int(_os.environ.get("MK_PHASES", "99"))
        phases = [(ph_init,)]
        for h in (0, 1):
            phases += [(ph_ln1, h), (ph_mamba, h), (ph_post, h)]
        phases.append((ph_gate,))
        for h in (0, 1):
            phases += [(ph_ffn, h), (ph_fin, h)]
        for ph in phases[:_plim]:
            run_phase(*ph)
        run_phase(ph_end)

    return nc


def kernel(L0_attn_x, L0_attn_W, L0_attn_b, L0_attn_A, L0_attn_D,
           L0_ffn_gate_weight, L0_ffn_W1, L0_ffn_b1, L0_ffn_W2, L0_ffn_b2,
           L0_ffn_Ws1, L0_ffn_bs1, L0_ffn_Ws2, L0_ffn_bs2):
    host = _host_prep(np.asarray(L0_attn_W), np.asarray(L0_attn_b),
                      np.asarray(L0_attn_A), np.asarray(L0_attn_D),
                      np.asarray(L0_ffn_gate_weight), np.asarray(L0_ffn_W1),
                      np.asarray(L0_ffn_b1), np.asarray(L0_ffn_W2),
                      np.asarray(L0_ffn_b2), np.asarray(L0_ffn_Ws1),
                      np.asarray(L0_ffn_bs1), np.asarray(L0_ffn_Ws2),
                      np.asarray(L0_ffn_bs2))
    nc = build_nc()
    x = np.ascontiguousarray(np.asarray(L0_attn_x))
    in_maps = []
    for c in range(8):
        m = dict(host)
        m["x"] = np.ascontiguousarray(x[2 * c: 2 * c + 2])
        in_maps.append(m)
    res = run_bass_kernel_spmd(nc, in_maps, core_ids=list(range(8)))
    return np.concatenate([r["out"] for r in res.results], axis=0)
